# version 15
# baseline (speedup 1.0000x reference)
import math
import numpy as np
import ml_dtypes
from contextlib import ExitStack

import concourse.bass as bass
import concourse.mybir as mybir
from concourse.bass_utils import run_bass_kernel_spmd

F32 = mybir.dt.float32
BF16 = mybir.dt.bfloat16
I32 = mybir.dt.int32
AF = mybir.ActivationFunctionType
ALU = mybir.AluOpType
AXX = mybir.AxisListType.X
BFNP = ml_dtypes.bfloat16

D = 2048
T = 8192
DH = 128
G = 4
NH = 16
DI = 8192
NCORE = 8
KB = 128
NKB = T // KB
QB = 256
NSLOT = 4
KS = [64, 48, 32, 16]
EPS_LN = 1e-5
EPS_RMS = 1e-6
ROT = 10000.0
NEG = -1e30
HC = D // 128
NIC = DI // 128

DBG = 0


def plan_core(c):
    slots = [24 + c, 16 + c, 8 + c, c]
    prog = [None] * NKB
    for i in range(2 * c):
        prog[i] = i
    prog[14] = 2 * c
    prog[15] = 2 * c + 1
    for k in range(1, 4):
        base = 2 * c + 2 + 16 * (k - 1)
        for i in range(16):
            prog[16 * k + i] = base + i
    for j, b in enumerate(slots):
        P = 2 * b + 2
        seen = [p for p in prog[:KS[j]] if p is not None]
        assert sorted(seen) == list(range(P))
        assert prog[KS[j] - 2] == P - 2 and prog[KS[j] - 1] == P - 1
    return slots, prog


def host_tables():
    d = np.arange(64, dtype=np.float64)
    ang = ROT ** (-d / 64.0)
    tt = np.arange(T, dtype=np.float64)
    rad = np.outer(ang, tt)
    cos_t = np.concatenate([np.cos(rad)] * 2, 0).astype(np.float32)
    sin_t = np.concatenate([-np.sin(rad), np.sin(rad)], 0).astype(np.float32)
    p = np.arange(128)[:, None]
    f = np.arange(QB)[None, :]
    maskA = np.where(p > f, NEG, 0.0).astype(np.float32)
    maskB = np.where(p + 128 > f, NEG, 0.0).astype(np.float32)
    return cos_t, sin_t, maskA, maskB


class Sem:
    __slots__ = ("h", "n", "name")

    def __init__(self, nc, name):
        self.h = nc.alloc_semaphore(name=name)
        self.n = 0
        self.name = name


class Sched:
    ENGS = ("sync", "scalar", "vector", "tensor", "gpsimd")
    AUTO_BRIDGE = ("scalar", "vector")

    def __init__(self, nc):
        self.nc = nc
        self.ops = []
        self.auto = {}
        self.last_inc = {}

    def op(self, eng, fn, waits=(), incs=()):
        assert eng in self.ENGS, eng
        waits = [w for w in waits if w is not None]
        incs = list(incs)
        assert len(incs) <= 1, "one sync update per instruction"
        if eng in self.AUTO_BRIDGE:
            prev = self.last_inc.get(eng)
            if prev is not None and prev not in waits:
                waits.append(prev)
            if not incs:
                if eng not in self.auto:
                    self.auto[eng] = Sem(self.nc, f"auto_{eng}")
                incs = [self.inc(self.auto[eng])]
            self.last_inc[eng] = (incs[0][0], incs[0][0].n)
        self.ops.append((eng, fn, tuple(waits), tuple(incs)))

    def inc(self, sem, k=1):
        sem.n += k
        return (sem, k)

    def emit(self, name):
        nc = self.nc
        with nc.Block(name) as block:
            for eng_name in self.ENGS:
                ops = [o for o in self.ops if o[0] == eng_name]
                if not ops:
                    continue

                def body(eng, ops=ops):
                    for (_, fn, waits, incs) in ops:
                        for (s, tgt) in waits:
                            if tgt > 0:
                                eng.wait_ge(s.h, tgt)
                        inst = fn(eng)
                        for (s, k) in incs:
                            inst.then_inc(s.h, k)
                getattr(block, eng_name)(body)
        self.ops = []


def rsqrt_newton(sc, gate, hop, v_ap, ta, tb, out_inv, inv_n, eps,
                 done_sem):
    I = sc.inc
    sc.op("vector", lambda e: e.tensor_scalar(ta, v_ap, inv_n, eps,
                                              ALU.mult, ALU.add),
          waits=gate, incs=[I(hop)])

    def lnexp(e):
        e.activation(tb, ta, AF.Ln)
        e.drain()
        return e.activation(out_inv, tb, AF.Exp, scale=-0.5)
    sc.op("scalar", lnexp, waits=[(hop, hop.n)], incs=[I(done_sem)])


def ln_pre(sc, nc, xap, bns_ap, mv_ap, st_ap, eps, bn_waits, hop):
    I = sc.inc

    def bnst(e):
        r = None
        for q in range(4):
            r = e.bn_stats(bns_ap[:, q * 6:(q + 1) * 6],
                           xap[:, q * 512:(q + 1) * 512])
        return r
    sc.op("vector", bnst, waits=bn_waits, incs=[])
    sc.op("vector", lambda e: e.bn_aggr(mv_ap, bns_ap), incs=[I(hop)])


def ln_post(sc, nc, xap, mv_ap, st_ap, out_ap, eps, apply_waits, done_incs,
            hop):
    I = sc.inc
    inv = st_ap[:, 0:1]
    mb = st_ap[:, 1:2]
    ta = st_ap[:, 2:3]
    tb = st_ap[:, 3:4]
    rsqrt_newton(sc, [(hop, hop.n)], hop, mv_ap[:, 1:2], ta, tb, inv, 1.0,
                 eps, hop)

    def mbias(e):
        e.tensor_tensor(mb, mv_ap[:, 0:1], inv, ALU.mult)
        e.drain()
        return e.tensor_scalar(mb, mb, -1.0, None, ALU.mult)
    sc.op("vector", mbias, waits=[(hop, hop.n)], incs=[I(hop)])
    sc.op("scalar", lambda e: e.activation(out_ap, xap, AF.Identity,
                                           bias=mb, scale=inv),
          waits=[(hop, hop.n)] + list(apply_waits), incs=list(done_incs))


def ln_tile(sc, nc, xap, bns_ap, mv_ap, st_ap, out_ap, eps, bn_waits,
            apply_waits, done_incs, hop):
    ln_pre(sc, nc, xap, bns_ap, mv_ap, st_ap, eps, bn_waits, hop)
    ln_post(sc, nc, xap, mv_ap, st_ap, out_ap, eps, apply_waits, done_incs,
            hop)


def build_nc(dbg=DBG, phases=("A1", "A2", "B", "C", "D")):
    nc = bass.Bass("TRN2", target_bir_lowering=False)
    es = ExitStack()

    def sb(name, shape, dt):
        return es.enter_context(nc.sbuf_tensor(name, shape, dt))

    def dram_in(name, shape, dt):
        return nc.dram_tensor(name, shape, dt, kind="ExternalInput").ap()

    def dram_scr(name, shape, dt):
        kind = "ExternalOutput" if dbg else "Internal"
        return nc.dram_tensor(name, shape, dt, kind=kind).ap()

    ctx = {}
    ctx["wq"] = dram_in("L0_attn_wq", [D, D], F32)
    ctx["wk"] = dram_in("L0_attn_wk", [D, 512], F32)
    ctx["wv"] = dram_in("L0_attn_wv", [D, 512], F32)
    ctx["wo"] = dram_in("L0_attn_wo", [D, D], F32)
    ctx["gw"] = dram_in("L0_ffn_gate_w", [DI, D], I32)
    ctx["uw"] = dram_in("L0_ffn_up_w", [DI, D], I32)
    ctx["dw"] = dram_in("L0_ffn_down_w", [D, DI], I32)
    ctx["gs"] = dram_in("L0_ffn_gate_scale", [DI], F32)
    ctx["us"] = dram_in("L0_ffn_up_scale", [DI], F32)
    ctx["ds"] = dram_in("L0_ffn_down_scale", [D], F32)
    ctx["gq_f"] = dram_in("gq_flat", [NH * DH], F32)
    ctx["gk_f"] = dram_in("gk_flat", [G * DH], F32)
    ctx["gk_r"] = dram_in("gk_rep", [NH * DH], F32)
    ctx["xown"] = dram_in("x_own", [2 * NSLOT, KB, D], F32)
    ctx["coso"] = dram_in("cos_own", [128, NSLOT * QB], F32)
    ctx["sino"] = dram_in("sin_own", [128, NSLOT * QB], F32)
    ctx["padrep"] = dram_in("pad_rep", [128, NSLOT * (NKB + 2)], F32)
    ctx["maskA_i"] = dram_in("maskA", [128, QB], F32)
    ctx["maskB_i"] = dram_in("maskB", [128, QB], F32)
    ctx["idf_i"] = dram_in("id_f32", [128, 128], F32)
    ctx["idb_i"] = dram_in("id_bf16", [128, 128], BF16)

    ctx["out"] = nc.dram_tensor("out_shard", [2 * NSLOT, KB, D], F32,
                                kind="ExternalOutput").ap()
    ctx["kv_own"] = nc.dram_tensor("kv_own", [G, 8, 128, 512], BF16,
                                   kind="Internal").ap()
    ctx["kv_gath"] = nc.dram_tensor("kv_gath", [G, 8, 4, 2, 128, 512],
                                    BF16, kind="Internal").ap()
    ctx["q_scr"] = dram_scr("q_scr", [NSLOT, NH, 128, 512], BF16)
    ctx["o_scr"] = dram_scr("o_scr", [HC, NSLOT, 128, 512], BF16)
    ctx["x2_scr"] = dram_scr("x2_scr", [2 * NSLOT, 128, D], F32)

    ctx["c_idf"] = sb("c_idf", [128, 128], F32)
    ctx["c_idb"] = sb("c_idb", [128, 128], BF16)
    ctx["c_maskA"] = sb("c_maskA", [128, QB], F32)
    ctx["c_maskB"] = sb("c_maskB", [128, QB], F32)
    ctx["c_bias"] = sb("c_bias", [128, NSLOT * (NKB + 2)], F32)
    ctx["c_ones"] = sb("c_ones", [128, 1], BF16)
    ctx["c_onesf"] = sb("c_onesf", [128, 1], F32)
    ctx["c_onesrf"] = sb("c_onesrf", [1, 128], F32)
    ctx["c_n128"] = sb("c_n128", [1, 128], BF16)
    ctx["c_gq"] = sb("c_gq", [1, NH * DH], F32)
    ctx["c_gk"] = sb("c_gk", [1, G * DH], F32)
    ctx["c_sg"] = sb("c_sg", [128, NIC], F32)
    ctx["c_su"] = sb("c_su", [128, NIC], F32)
    ctx["c_sd"] = sb("c_sd", [128, HC], F32)
    ctx["c_crep"] = sb("c_crep", [128, 1], F32)
    ctx["c_tmp"] = sb("c_tmp", [128, NSLOT * (NKB + 2)], F32)

    sc = Sched(nc)
    build_setup(nc, sc, ctx)
    if "A1" in phases:
        build_a1(nc, sc, ctx)
    if "A2" in phases:
        build_a2(nc, sc, ctx)
    if "B" in phases:
        build_b(nc, sc, ctx)
    if "C" in phases:
        build_c(nc, sc, ctx)
    if "D" in phases:
        build_d(nc, sc, ctx)
    es.close()
    return nc


def build_setup(nc, sc, ctx):
    I = sc.inc
    s_ld = Sem(nc, "su_ld")
    s_scl = Sem(nc, "su_scl")
    s_c = Sem(nc, "su_c")
    with nc.psum_tensor("p_set", [128, 512], F32) as p_set, \
            nc.sbuf_tensor("su_gkr", [1, NH * DH], F32) as gkr:
        tmp = ctx["c_tmp"]
        loads = [
            (ctx["c_idf"][:], ctx["idf_i"][:, :]),
            (ctx["c_idb"][:], ctx["idb_i"][:, :]),
            (ctx["c_maskA"][:], ctx["maskA_i"][:, :]),
            (ctx["c_maskB"][:], ctx["maskB_i"][:, :]),
            (tmp[:], ctx["padrep"][:, :]),
            (ctx["c_gq"][:], ctx["gq_f"].rearrange("(o n) -> o n", o=1)),
            (ctx["c_gk"][:], ctx["gk_f"].rearrange("(o n) -> o n", o=1)),
            (gkr[:, :], ctx["gk_r"].rearrange("(o n) -> o n", o=1)),
        ]
        for dst, src in loads:
            sc.op("sync", lambda e, d=dst, s=src: e.dma_start(out=d, in_=s),
                  incs=[I(s_ld, 16)])
        for src, dst, n in ((ctx["gs"], ctx["c_sg"], DI),
                            (ctx["us"], ctx["c_su"], DI),
                            (ctx["ds"], ctx["c_sd"], D)):
            sc.op("scalar", lambda e, s=src, d=dst, n=n: e.dma_start(
                out=d[:, 0:n // 128],
                in_=s.rearrange("(c p) -> p c", p=128),
                allow_slow_non_contiguous=True), incs=[I(s_scl, 16)])
        ctx["scl_tok"] = (s_scl, s_scl.n)
        sc.op("gpsimd", lambda e: e.memset(ctx["c_ones"][:], 1.0),
              incs=[I(s_c)])
        sc.op("gpsimd", lambda e: e.memset(ctx["c_onesf"][:], 1.0),
              incs=[I(s_c)])
        sc.op("gpsimd", lambda e: e.memset(ctx["c_onesrf"][:], 1.0),
              incs=[I(s_c)])
        sc.op("gpsimd", lambda e: e.memset(ctx["c_n128"][:], -128.0),
              incs=[I(s_c)])
        nld = s_ld.n

        def cstar(e):
            e.tensor_tensor(gkr[:, :], ctx["c_gq"][:, :], gkr[:, :],
                            ALU.mult)
            e.drain()
            e.tensor_reduce(ctx["c_crep"][0:1, 0:1], gkr[:, :],
                            AXX, ALU.max, apply_absolute_value=True)
            e.drain()
            return e.tensor_scalar(ctx["c_crep"][0:1, 0:1],
                                   ctx["c_crep"][0:1, 0:1],
                                   math.sqrt(128.0), None, ALU.mult)
        sc.op("vector", cstar, waits=[(s_ld, nld)], incs=[I(s_c)])
        sc.op("tensor", lambda e: e.matmul(
            p_set[:, 0:1], ctx["c_onesrf"][:, :], ctx["c_crep"][0:1, 0:1],
            start=True, stop=True), waits=[(s_c, s_c.n)], incs=[I(s_c)])
        sc.op("vector", lambda e: e.tensor_copy(ctx["c_crep"][:, :],
                                                p_set[:, 0:1]),
              waits=[(s_c, s_c.n)], incs=[I(s_c)])
        sc.op("vector", lambda e: e.tensor_scalar(
            ctx["c_bias"][:, :], tmp[:, :], NEG, ctx["c_crep"][:, 0:1],
            ALU.mult, ALU.subtract), waits=[(s_c, s_c.n)], incs=[I(s_c)])
        sc.emit("setup")


def build_a1(nc, sc, ctx):
    I = sc.inc
    st = ExitStack()

    def sbl(n, s, d):
        return st.enter_context(nc.sbuf_tensor(n, s, d))

    def psl(n, s, d):
        return st.enter_context(nc.psum_tensor(n, s, d))

    xt = sbl("a1_xt", [128, 2 * D], F32)
    trig = sbl("a1_trig", [128, 4 * 256], F32)
    bns = sbl("a1_bns", [128, 2 * 24], F32)
    mv = sbl("a1_mv", [128, 2 * 2], F32)
    stt = sbl("a1_stt", [128, 2 * 4], F32)
    lnc = sbl("a1_lnc", [128, 4 * D], F32)
    xfh = sbl("a1_xfh", [128, 2 * 4096], BF16)
    xfl = sbl("a1_xfl", [128, 2 * 4096], BF16)
    wkvh = sbl("a1_wkvh", [128, HC * 1024], BF16)
    wkvl = sbl("a1_wkvl", [128, HC * 1024], BF16)
    wldf = sbl("a1_wldf", [128, 2 * 1024], F32)
    kp = sbl("a1_kp", [128, 8 * 256], F32)
    ksqb = sbl("a1_ksqb", [128, 256], F32)
    vst = sbl("a1_vst", [128, 4 * 256], BF16)
    rr = sbl("a1_rr", [1, 3 * 256], F32)
    kvo = sbl("a1_kvo", [128, 2 * G * 1024], BF16)

    pxT = [psl("a1_pxT0", [128, 512], F32), psl("a1_pxT1", [128, 512], F32)]
    pkv = [psl(f"a1_pkv{i}", [128, 512], F32) for i in range(4)]
    prow = psl("a1_prow", [128, 512], F32)
    pvT = psl("a1_pvT", [128, 1024], BF16)

    s_ldw = [Sem(nc, "a1_ldw0"), Sem(nc, "a1_ldw1")]
    s_wsp = Sem(nc, "a1_wsp")
    s_ldx = [Sem(nc, f"a1_ldx{i}") for i in range(4)]
    s_trl = [Sem(nc, "a1_trl0"), Sem(nc, "a1_trl1")]
    hop = Sem(nc, "a1_hop")
    s_ln = Sem(nc, "a1_ln")
    s_xT = Sem(nc, "a1_xT")
    s_xfh = Sem(nc, "a1_xfh")
    s_xfl = Sem(nc, "a1_xfl")
    s_kv = Sem(nc, "a1_kv")
    s_pe = Sem(nc, "a1_pe")
    s_kc = Sem(nc, "a1_kc")
    s_ke = Sem(nc, "a1_ke")
    s_sw = Sem(nc, "a1_sw")
    s_stg = Sem(nc, "a1_stg")
    s_sto = [Sem(nc, "a1_sto0"), Sem(nc, "a1_sto1")]
    hop2 = Sem(nc, "a1_hop2")

    wldf_free = {}
    for hc in range(HC):
        b = hc % 2
        w = [wldf_free.get(hc - 2)]
        sc.op("sync", lambda e, hc=hc, b=b: e.dma_start(
            out=wldf[:, b * 1024:b * 1024 + 512],
            in_=ctx["wk"][hc * 128:(hc + 1) * 128, :]), waits=w,
            incs=[I(s_ldw[b], 16)])
        sc.op("sync", lambda e, hc=hc, b=b: e.dma_start(
            out=wldf[:, b * 1024 + 512:b * 1024 + 1024],
            in_=ctx["wv"][hc * 128:(hc + 1) * 128, :]),
            incs=[I(s_ldw[b], 16)])
        sc.op("vector", lambda e, hc=hc, b=b: e.tensor_copy(
            wkvh[:, hc * 1024:(hc + 1) * 1024],
            wldf[:, b * 1024:(b + 1) * 1024]),
            waits=[(s_ldw[b], s_ldw[b].n)], incs=[I(s_wsp)])
        sc.op("vector", lambda e, hc=hc, b=b: e.tensor_tensor(
            wkvl[:, hc * 1024:(hc + 1) * 1024],
            wldf[:, b * 1024:(b + 1) * 1024],
            wkvh[:, hc * 1024:(hc + 1) * 1024], ALU.subtract),
            incs=[I(s_wsp)])
        wldf_free[hc] = (s_wsp, s_wsp.n)
    wsp_done = s_wsp.n

    NSB = NSLOT
    ln_tok = []
    xT_tok = []
    rope_tok = []
    bankfree = {}
    stg_tok = []
    sto_tok = []
    ctxA = dict(xt=xt, trig=trig, kp=kp, ksqb=ksqb, vst=vst, rr=rr, kvo=kvo,
                pkv=pkv, prow=prow, pvT=pvT, pxT=pxT, s_kv=s_kv, s_pe=s_pe,
                s_kc=s_kc, s_ke=s_ke, s_sw=s_sw, s_stg=s_stg,
                bankfree=bankfree, hop2=hop2, s_trl=s_trl)
    trl_toks = {}

    def emit_loads(s):
        pp = s % 2
        for ti in range(2):
            w = [(s_ln, ln_tok[s - 1][1]) if s >= 1 else None]
            sc.op("sync", lambda e, s=s, ti=ti, pp=pp: e.dma_start(
                out=xt[:, ti * D:(ti + 1) * D],
                in_=ctx["xown"][2 * s + ti, :, :]), waits=w,
                incs=[I(s_ldx[pp * 2 + ti], 16)])
        w = [rope_tok[s - 2] if s >= 2 else None]
        sc.op("sync", lambda e, s=s, pp=pp: e.dma_start(
            out=trig[:, pp * 512:pp * 512 + 256],
            in_=ctx["coso"][:, s * 256:(s + 1) * 256]), waits=w,
            incs=[I(s_trl[pp], 16)])
        sc.op("sync", lambda e, s=s, pp=pp: e.dma_start(
            out=trig[:, pp * 512 + 256:pp * 512 + 512],
            in_=ctx["sino"][:, s * 256:(s + 1) * 256]),
            incs=[I(s_trl[pp], 16)])
        trl_toks[s] = (s_trl[pp], s_trl[pp].n)

    def emit_ln(s, stage):
        pp = s % 2
        for ti in range(2):
            xap = xt[:, ti * D:(ti + 1) * D]
            if stage == 0:
                bn_w = [(s_ldx[pp * 2 + ti], s_ldx[pp * 2 + ti].n)]
                ln_pre(sc, nc, xap, bns[:, ti * 24:(ti + 1) * 24],
                       mv[:, ti * 2:(ti + 1) * 2],
                       stt[:, ti * 4:(ti + 1) * 4], EPS_LN, bn_w, hop)
            else:
                ap_w = [(s_xT, xT_tok[s - 2][1]) if s >= 2 else None]
                ap_w = [w for w in ap_w if w]
                ln_post(sc, nc, xap, mv[:, ti * 2:(ti + 1) * 2],
                        stt[:, ti * 4:(ti + 1) * 4],
                        lnc[:, (pp * 2 + ti) * D:(pp * 2 + ti + 1) * D],
                        EPS_LN, ap_w, [I(s_ln)], hop)
        if stage == 1:
            ln_tok.append((s_ln, s_ln.n))

    emit_loads(0)
    emit_ln(0, 0)
    emit_ln(0, 1)
    for s in range(NSB):
        pp = s % 2
        ctxA["trl_tok"] = trl_toks[s]
        if s + 1 < NSB:
            emit_loads(s + 1)

        for r in range(8):
            rg = s * 8 + r

            def tr(e, r=r, pp=pp):
                rr_ = None
                for k in range(4):
                    hc = 2 * r + (k // 2)
                    ti = k % 2
                    rr_ = e.transpose(pxT[r % 2][:, k * 128:(k + 1) * 128],
                                      lnc[:, (pp * 2 + ti) * D + hc * 128:
                                          (pp * 2 + ti) * D + (hc + 1) * 128],
                                      ctx["c_idf"][:, :])
                return rr_
            w = [(s_ln, s_ln.n)]
            if rg >= 2:
                w.append((s_xfh, rg - 1))
                w.append((s_xfl, rg - 1))
            if r < 2 and s >= 1:
                w.append(rope_tok[s - 1])
            sc.op("tensor", tr, waits=w, incs=[I(s_xT)])
            dst = pp * 4096 + r * 512
            sc.op("scalar", lambda e, r=r, dst=dst: e.activation(
                xfh[:, dst:dst + 512], pxT[r % 2][:, :], AF.Copy, bias=0.0,
                scale=1.0), waits=[(s_xT, s_xT.n)], incs=[I(s_xfh)])
            sc.op("vector", lambda e, r=r, dst=dst: e.tensor_tensor(
                xfl[:, dst:dst + 512], pxT[r % 2][:, :],
                xfh[:, dst:dst + 512], ALU.subtract),
                waits=[(s_xfh, s_xfh.n)], incs=[I(s_xfl)])
        xT_tok.append((s_xT, s_xT.n))
        xf_done = [(s_xfh, s_xfh.n), (s_xfl, s_xfl.n)]
        if s + 1 < NSB:
            emit_ln(s + 1, 0)
            emit_ln(s + 1, 1)

        posts = []

        def stage1(i):
            poc, pgidx, ptok, pgate = posts[i]
            ctxA["kv_tok"] = ptok
            if poc < G:
                _a1_kpost1(nc, sc, ctx, ctxA, s, pp, poc, pgidx)
            else:
                _a1_vpost1(nc, sc, ctx, ctxA, s, pp, poc, pgidx)

        def stage2(i):
            poc, pgidx, ptok, pgate = posts[i]
            if poc < G:
                _a1_kpost2(nc, sc, ctx, ctxA, s, pp, poc, pgidx, pgate)
            else:
                _a1_vpost2(nc, sc, ctx, ctxA, s, pp, poc, pgidx, pgate)

        for oc in range(8):
            gidx = s * 8 + oc
            bank = pkv[oc % 4]

            def kvmm(e, oc=oc, bank=bank, pp=pp):
                r = None
                for hc in range(HC):
                    ws = slice(hc * 1024 + oc * 128,
                               hc * 1024 + (oc + 1) * 128)
                    xs = slice(pp * 4096 + hc * 256,
                               pp * 4096 + (hc + 1) * 256)
                    e.matmul(bank[:, 0:256], wkvh[:, ws], xfh[:, xs],
                             start=(hc == 0), stop=False)
                    e.matmul(bank[:, 0:256], wkvl[:, ws], xfh[:, xs],
                             start=False, stop=False)
                    r = e.matmul(bank[:, 0:256], wkvh[:, ws], xfl[:, xs],
                                 start=False, stop=(hc == HC - 1))
                return r
            w = list(xf_done) + [(s_wsp, wsp_done)]
            w += bankfree.get(gidx - 4, [])
            sc.op("tensor", kvmm, waits=w, incs=[I(s_kv)])
            stg_gate = sto_tok[s - 2] if s >= 2 else None
            posts.append((oc, gidx, s_kv.n, stg_gate))
            if oc >= 1:
                stage1(oc - 1)
            if oc >= 2:
                stage2(oc - 2)
        stage1(7)
        stage2(6)
        stage2(7)
        rope_tok.append((s_kc, s_kc.n))
        stg_tok.append((s_stg, s_stg.n))

        for g in range(G):
            base = pp * (G * 1024) + g * 1024
            w = [(s_stg, s_stg.n)]
            sc.op("gpsimd", lambda e, s=s, g=g, base=base: e.dma_start(
                out=ctx["kv_own"][g, 6 - 2 * s:8 - 2 * s, :, :].rearrange(
                    "b p f -> p b f"),
                in_=kvo[:, base:base + 1024].rearrange(
                    "p (b f) -> p b f", b=2)), waits=w,
                incs=[I(s_sto[pp], 16)])
        sto_tok.append((s_sto[pp], s_sto[pp].n))

    s_cc = Sem(nc, "a1_cc")
    ctx["s_cc"] = s_cc
    for g in range(G):
        w = ([(s_sto[0], s_sto[0].n), (s_sto[1], s_sto[1].n)]
             if g == 0 else [])
        sc.op("gpsimd", lambda e, g=g: e.collective_compute(
            "AllGather", ALU.bypass,
            replica_groups=[list(range(NCORE))],
            ins=[ctx["kv_own"][g, :, :, :]],
            outs=[ctx["kv_gath"][g].rearrange("c m b p f -> (c m b) p f")]),
            waits=w, incs=[I(s_cc)])

    sc.op("sync", lambda e: e.nop(),
          waits=[(s_sto[0], s_sto[0].n), (s_sto[1], s_sto[1].n)])
    sc.emit("A1")
    st.close()


def _a1_kpost1(nc, sc, ctx, A, s, pp, g, gidx):
    I = sc.inc
    bank = A["pkv"][g % 4]
    kp, ksqb, prow = A["kp"], A["ksqb"], A["prow"]
    s_kv, s_pe, s_kc, s_ke = A["s_kv"], A["s_pe"], A["s_kc"], A["s_ke"]
    kf = kp[:, (g % 2) * 256:(g % 2) * 256 + 256]
    sc.op("scalar", lambda e, kf=kf: e.activation(
        kf, bank[:, 0:256], AF.Copy, bias=0.0, scale=1.0),
        waits=[(s_kv, A["kv_tok"]), (s_kc, s_kc.n)], incs=[I(s_ke)])
    A["bankfree"][gidx] = [(s_ke, s_ke.n)]
    sc.op("vector", lambda e, kf=kf: e.tensor_tensor(ksqb[:, :], kf, kf,
                                                     ALU.mult),
          waits=[(s_ke, s_ke.n), (s_pe, s_pe.n)], incs=[I(s_kc)])
    pxT = A["pxT"]
    sc.op("tensor", lambda e, g=g: e.matmul(
        pxT[g % 2][0:1, 256:512], ctx["c_onesf"][:, :], ksqb[:, :],
        start=True, stop=True),
        waits=[(s_kc, s_kc.n)], incs=[I(s_pe)])
    A["ssq_tok" + str(g % 2)] = (s_pe, s_pe.n)


def _a1_kpost2(nc, sc, ctx, A, s, pp, g, gidx, stg_gate):
    I = sc.inc
    kp, rr, trig, kvo, prow = (A["kp"], A["rr"], A["trig"], A["kvo"],
                               A["prow"])
    s_pe, s_kc, s_ke, s_sw, s_stg = (A["s_pe"], A["s_kc"], A["s_ke"],
                                     A["s_sw"], A["s_stg"])
    hop2 = A["hop2"]
    pxT = A["pxT"]
    kf = kp[:, (g % 2) * 256:(g % 2) * 256 + 256]
    bc = kp[:, 512:768]
    kh = kp[:, 768:1024]
    ksw = kp[:, 1024:1280]
    t1 = kp[:, 1280:1536]
    krot = kp[:, 1536:1792]
    rsqrt_newton(sc, [A["ssq_tok" + str(g % 2)]], hop2,
                 pxT[g % 2][0:1, 256:512],
                 rr[0:1, 0:256], rr[0:1, 256:512], rr[0:1, 512:768],
                 1.0 / DH, EPS_RMS, s_kc)
    sc.op("tensor", lambda e, g=g: e.matmul(
        pxT[g % 2][:, 0:256], ctx["c_gk"][0:1, g * 128:(g + 1) * 128],
        rr[0:1, 512:768], start=True, stop=True),
        waits=[(s_kc, s_kc.n)], incs=[I(s_pe)])
    sc.op("vector", lambda e, g=g: e.tensor_copy(bc, pxT[g % 2][:, 0:256]),
          waits=[(s_pe, s_pe.n)], incs=[I(s_kc)])
    sc.op("vector", lambda e, kf=kf: e.tensor_tensor(kh, kf, bc, ALU.mult),
          waits=[(s_sw, s_sw.n)], incs=[I(s_kc)])
    sc.op("gpsimd", lambda e: e.dma_start(out=ksw[0:64, :],
                                          in_=kh[64:128, :]),
          waits=[(s_kc, s_kc.n)], incs=[I(s_sw, 16)])
    sc.op("gpsimd", lambda e: e.dma_start(out=ksw[64:128, :],
                                          in_=kh[0:64, :]),
          incs=[I(s_sw, 16)])

    def rope(e, pp=pp):
        e.tensor_tensor(t1, kh, trig[:, pp * 512:pp * 512 + 256], ALU.mult)
        e.tensor_tensor(krot, ksw, trig[:, pp * 512 + 256:pp * 512 + 512],
                        ALU.mult)
        e.drain()
        return e.tensor_tensor(krot, krot, t1, ALU.add)
    sc.op("vector", rope, waits=[(s_sw, s_sw.n), A["trl_tok"]],
          incs=[I(s_kc)])

    def split(e, pp=pp, g=g):
        for b in range(2):
            base = pp * (G * 1024) + g * 1024 + b * 512
            e.tensor_copy(kvo[:, base:base + 128],
                          krot[:, b * 128:(b + 1) * 128])
        e.drain()
        r = None
        for b in range(2):
            base = pp * (G * 1024) + g * 1024 + b * 512
            r = e.tensor_tensor(kvo[:, base + 128:base + 256],
                                krot[:, b * 128:(b + 1) * 128],
                                kvo[:, base:base + 128], ALU.subtract)
        return r
    sc.op("vector", split, waits=[stg_gate] if stg_gate else [],
          incs=[I(s_stg)])


def _a1_vpost1(nc, sc, ctx, A, s, pp, oc, gidx):
    I = sc.inc
    bank = A["pkv"][oc % 4]
    vst = A["vst"]
    s_kv, s_pe, s_kc, s_ke = A["s_kv"], A["s_pe"], A["s_kc"], A["s_ke"]
    vo = (oc % 2) * 512
    vh = vst[:, vo:vo + 256]
    vl = vst[:, vo + 256:vo + 512]
    sc.op("scalar", lambda e, vh=vh: e.activation(
        vh, bank[:, 0:256], AF.Copy, bias=0.0, scale=1.0),
        waits=[(s_kv, A["kv_tok"]), (s_pe, s_pe.n), (s_kc, s_kc.n)],
        incs=[I(s_ke)])
    sc.op("vector", lambda e, vh=vh, vl=vl: e.tensor_tensor(
        vl, bank[:, 0:256], vh, ALU.subtract),
        waits=[(s_ke, s_ke.n)], incs=[I(s_kc)])
    A["bankfree"][gidx] = [(s_ke, s_ke.n), (s_kc, s_kc.n)]
    A["v_tok" + str(oc % 2)] = (s_kc, s_kc.n)


def _a1_vpost2(nc, sc, ctx, A, s, pp, oc, gidx, stg_gate):
    I = sc.inc
    g = oc - G
    vst, pvT, kvo = A["vst"], A["pvT"], A["kvo"]
    s_pe, s_kc, s_stg = A["s_pe"], A["s_kc"], A["s_stg"]
    vo = (oc % 2) * 512
    vh = vst[:, vo:vo + 256]
    vl = vst[:, vo + 256:vo + 512]

    def trs(e, vh=vh, vl=vl):
        r = None
        for b in range(2):
            r = e.transpose(pvT[:, (2 * b) * 128:(2 * b + 1) * 128],
                            vh[:, b * 128:(b + 1) * 128], ctx["c_idb"][:, :])
            r = e.transpose(pvT[:, (2 * b + 1) * 128:(2 * b + 2) * 128],
                            vl[:, b * 128:(b + 1) * 128], ctx["c_idb"][:, :])
        return r
    sc.op("tensor", trs, waits=[A["v_tok" + str(oc % 2)],
                                (A["s_stg"], A["s_stg"].n)], incs=[I(s_pe)])

    def evac(e, pp=pp, g=g):
        r = None
        for b in range(2):
            base = pp * (G * 1024) + g * 1024 + b * 512 + 256
            e.tensor_copy(kvo[:, base:base + 128],
                          pvT[:, (2 * b) * 128:(2 * b + 1) * 128])
            r = e.tensor_copy(kvo[:, base + 128:base + 256],
                              pvT[:, (2 * b + 1) * 128:(2 * b + 2) * 128])
        return r
    w = [(s_pe, s_pe.n)]
    if stg_gate:
        w.append(stg_gate)
    sc.op("vector", evac, waits=w, incs=[I(s_stg)])



def build_a2(nc, sc, ctx):
    I = sc.inc
    st = ExitStack()

    def sbl(n, s, d):
        return st.enter_context(nc.sbuf_tensor(n, s, d))

    def psl(n, s, d):
        return st.enter_context(nc.psum_tensor(n, s, d))

    xt = sbl("a2_xt", [128, 2 * D], F32)
    trig = sbl("a2_trig", [128, 2 * 512], F32)
    bns = sbl("a2_bns", [128, 24], F32)
    mv = sbl("a2_mv", [128, 2], F32)
    stt = sbl("a2_stt", [128, 4], F32)
    lnc = sbl("a2_lnc", [128, 2 * D], F32)
    xqh = sbl("a2_xqh", [128, HC * 1024], BF16)
    xql = sbl("a2_xql", [128, HC * 1024], BF16)
    wqh = sbl("a2_wqh", [128, HC * 1024], BF16)
    wql = sbl("a2_wql", [128, HC * 1024], BF16)
    wldf = sbl("a2_wldf", [128, 2 * 1024], F32)
    kp = sbl("a2_kp", [128, 8 * 256], F32)
    ksqb = sbl("a2_ksqb", [128, 256], F32)
    rr = sbl("a2_rr", [1, 3 * 256], F32)
    qvo = sbl("a2_qvo", [128, 2 * 512], BF16)

    pxT = [psl("a2_pxT0", [128, 512], F32), psl("a2_pxT1", [128, 512], F32)]
    pq = [psl(f"a2_pq{i}", [128, 512], F32) for i in range(4)]
    qb = [psl("a2_qb0", [128, 512], F32), psl("a2_qb1", [128, 512], F32)]

    s_ldx = [Sem(nc, "a2_ldx0"), Sem(nc, "a2_ldx1")]
    s_trg = [Sem(nc, "a2_trg0"), Sem(nc, "a2_trg1")]
    hop = Sem(nc, "a2_hop")
    s_ln = Sem(nc, "a2_ln")
    s_xT = Sem(nc, "a2_xT")
    s_xfh = Sem(nc, "a2_xfh")
    s_xfl = Sem(nc, "a2_xfl")
    s_ldw = [Sem(nc, "a2_ldw0"), Sem(nc, "a2_ldw1")]
    s_wsp = Sem(nc, "a2_wsp")
    s_q = Sem(nc, "a2_q")
    s_pe = Sem(nc, "a2_pe")
    s_kc = Sem(nc, "a2_kc")
    s_ke = Sem(nc, "a2_ke")
    s_sw = Sem(nc, "a2_sw")
    s_stg = Sem(nc, "a2_stg")
    s_sto = [Sem(nc, "a2_sto0"), Sem(nc, "a2_sto1")]
    hop2 = Sem(nc, "a2_hop2")

    xT_tok = []
    ln_tok = []

    def emit_prep(tl):
        b = tl % 2
        w = [(s_ln, ln_tok[tl - 2][1]) if tl >= 2 else None]
        sc.op("sync", lambda e, tl=tl, b=b: e.dma_start(
            out=xt[:, b * D:(b + 1) * D], in_=ctx["xown"][tl, :, :]),
            waits=w, incs=[I(s_ldx[b], 16)])
        bn_w = [(s_ldx[b], s_ldx[b].n)]
        ap_w = [xT_tok[tl - 2] if tl >= 2 else None]
        ap_w = [w_ for w_ in ap_w if w_]
        ln_tile(sc, nc, xt[:, b * D:(b + 1) * D], bns[:, :], mv[:, :],
                stt[:, :], lnc[:, b * D:(b + 1) * D], EPS_LN, bn_w, ap_w,
                [I(s_ln)], hop)
        ln_tok.append((s_ln, s_ln.n))
        for r in range(8):
            rg = tl * 8 + r

            def tr(e, r=r, b=b):
                rr_ = None
                for k in range(2):
                    hc = 2 * r + k
                    rr_ = e.transpose(
                        pxT[(tl * 8 + r) % 2][:, k * 128:(k + 1) * 128],
                        lnc[:, b * D + hc * 128:b * D + (hc + 1) * 128],
                        ctx["c_idf"][:, :])
                return rr_
            w = [(s_ln, s_ln.n)]
            if rg >= 2:
                w.append((s_xfh, rg - 1))
                w.append((s_xfl, rg - 1))
            sc.op("tensor", tr, waits=w, incs=[I(s_xT)])

            def ehi(e, r=r, tl=tl):
                rr_ = None
                for k in range(2):
                    hc = 2 * r + k
                    dst = hc * 1024 + (tl // 2) * 256 + (tl % 2) * 128
                    rr_ = e.activation(xqh[:, dst:dst + 128],
                                       pxT[(tl * 8 + r) % 2][:, k * 128:
                                                             (k + 1) * 128],
                                       AF.Copy, bias=0.0, scale=1.0)
                return rr_
            sc.op("scalar", ehi, waits=[(s_xT, s_xT.n)], incs=[I(s_xfh)])

            def elo(e, r=r, tl=tl):
                rr_ = None
                for k in range(2):
                    hc = 2 * r + k
                    dst = hc * 1024 + (tl // 2) * 256 + (tl % 2) * 128
                    rr_ = e.tensor_tensor(
                        xql[:, dst:dst + 128],
                        pxT[(tl * 8 + r) % 2][:, k * 128:(k + 1) * 128],
                        xqh[:, dst:dst + 128], ALU.subtract)
                return rr_
            sc.op("vector", elo, waits=[(s_xfh, s_xfh.n)],
                  incs=[I(s_xfl)])
        xT_tok.append((s_xT, s_xT.n))

    xf_gate = {}
    emit_prep(0)
    emit_prep(1)
    xf_gate[0] = [(s_xfh, s_xfh.n), (s_xfl, s_xfl.n)]

    bankfree = {}
    sto_tok = {}
    stg_tok = {}
    qpost_i = 0
    for half in range(2):
        wldf_free = {}
        half_gate = (s_q, s_q.n) if half else None
        for hc in range(HC):
            b = hc % 2
            w = [wldf_free.get(hc - 2)]
            if half_gate:
                w.append(half_gate)
            sc.op("sync", lambda e, half=half, hc=hc, b=b: e.dma_start(
                out=wldf[:, b * 1024:(b + 1) * 1024],
                in_=ctx["wq"][hc * 128:(hc + 1) * 128,
                              half * 1024:(half + 1) * 1024]), waits=w,
                incs=[I(s_ldw[b], 16)])
            sc.op("vector", lambda e, hc=hc, b=b: e.tensor_copy(
                wqh[:, hc * 1024:(hc + 1) * 1024],
                wldf[:, b * 1024:(b + 1) * 1024]),
                waits=[(s_ldw[b], s_ldw[b].n)], incs=[I(s_wsp)])
            sc.op("vector", lambda e, hc=hc, b=b: e.tensor_tensor(
                wql[:, hc * 1024:(hc + 1) * 1024],
                wldf[:, b * 1024:(b + 1) * 1024],
                wqh[:, hc * 1024:(hc + 1) * 1024], ALU.subtract),
                incs=[I(s_wsp)])
            wldf_free[hc] = (s_wsp, s_wsp.n)
        wsp_half = s_wsp.n

        post_q = []
        A2d = dict(kp=kp, ksqb=ksqb, rr=rr, trig=trig, qvo=qvo, qb=qb,
                   s_q=s_q, s_pe=s_pe, s_kc=s_kc, s_ke=s_ke,
                   s_sw=s_sw, s_stg=s_stg, s_sto=s_sto, s_trg=s_trg,
                   bankfree=bankfree, sto_tok=sto_tok, hop2=hop2)

        def pop_post(stage):
            if stage == 1:
                (poc, pslot, pidx, pbank, ptok) = post_q[-2]
                A2d["bank"] = pbank
                A2d["q_tok"] = ptok
                _a2_qpost1(nc, sc, ctx, A2d, poc, pslot, pidx)
            else:
                (poc, pslot, pidx, pbank, ptok) = post_q.pop(0)
                _a2_qpost2(nc, sc, ctx, A2d, poc, pslot, pidx)

        for slot in range(NSLOT):
            if half == 0 and slot + 1 < NSLOT:
                emit_prep(2 * (slot + 1))
                emit_prep(2 * (slot + 1) + 1)
                xf_gate[slot + 1] = [(s_xfh, s_xfh.n), (s_xfl, s_xfl.n)]
            for oci in range(8):
                oc = half * 8 + oci
                idx = qpost_i
                qpost_i += 1
                bank = pq[idx % 4]

                def qmm(e, oci=oci, slot=slot, bank=bank):
                    r = None
                    for hc in range(HC):
                        ws = slice(hc * 1024 + oci * 128,
                                   hc * 1024 + (oci + 1) * 128)
                        xs = slice(hc * 1024 + slot * 256,
                                   hc * 1024 + (slot + 1) * 256)
                        e.matmul(bank[:, 0:256], wqh[:, ws], xqh[:, xs],
                                 start=(hc == 0), stop=False)
                        e.matmul(bank[:, 0:256], wql[:, ws], xqh[:, xs],
                                 start=False, stop=False)
                        r = e.matmul(bank[:, 0:256], wqh[:, ws], xql[:, xs],
                                     start=False, stop=(hc == HC - 1))
                    return r
                w = list(xf_gate[slot]) + [(s_wsp, wsp_half)]
                w += bankfree.get(idx - 4, [])
                sc.op("tensor", qmm, waits=w, incs=[I(s_q)])
                post_q.append((oc, slot, idx, bank, s_q.n))
                if len(post_q) >= 2:
                    pop_post(1)
                if len(post_q) >= 3:
                    pop_post(2)
        post_q.append(post_q[-1])
        pop_post(1)
        pop_post(2)
        pop_post(2)
        post_q.clear()

    sc.op("sync", lambda e: e.nop(),
          waits=[(s_sto[0], s_sto[0].n), (s_sto[1], s_sto[1].n)])
    sc.emit("A2")
    st.close()


def rg_mod2(x):
    return x % 2


def _a2_qpost1(nc, sc, ctx, A, oc, slot, idx):
    I = sc.inc
    kp, ksqb, qb, bank = A["kp"], A["ksqb"], A["qb"], A["bank"]
    s_q, s_pe, s_kc, s_ke = A["s_q"], A["s_pe"], A["s_kc"], A["s_ke"]
    kf = kp[:, (idx % 2) * 256:(idx % 2) * 256 + 256]
    sc.op("scalar", lambda e, kf=kf: e.activation(
        kf, bank[:, 0:256], AF.Copy, bias=0.0, scale=1.0),
        waits=[(s_q, A["q_tok"]), (s_kc, s_kc.n)], incs=[I(s_ke)])
    A["bankfree"][idx] = [(s_ke, s_ke.n)]
    sc.op("vector", lambda e, kf=kf: e.tensor_tensor(ksqb[:, :], kf, kf,
                                                     ALU.mult),
          waits=[(s_ke, s_ke.n), (s_pe, s_pe.n)], incs=[I(s_kc)])
    sc.op("tensor", lambda e, idx=idx: e.matmul(
        qb[idx % 2][0:1, 256:512], ctx["c_onesf"][:, :], ksqb[:, :],
        start=True, stop=True), waits=[(s_kc, s_kc.n)], incs=[I(s_pe)])
    A["ssq_tok" + str(idx % 2)] = (s_pe, s_pe.n)


def _a2_qpost2(nc, sc, ctx, A, oc, slot, idx):
    I = sc.inc
    kp, rr, trig, qvo, qb = (A["kp"], A["rr"], A["trig"], A["qvo"],
                             A["qb"])
    s_pe, s_kc, s_ke, s_sw, s_stg, s_sto, s_trg = (
        A["s_pe"], A["s_kc"], A["s_ke"], A["s_sw"], A["s_stg"], A["s_sto"],
        A["s_trg"])
    hop2 = A["hop2"]
    kf = kp[:, (idx % 2) * 256:(idx % 2) * 256 + 256]
    bc = kp[:, 512:768]
    kh = kp[:, 768:1024]
    ksw = kp[:, 1024:1280]
    t1 = kp[:, 1280:1536]
    krot = kp[:, 1536:1792]
    pp = idx % 2
    ro = pp * 256

    sc.op("sync", lambda e, slot=slot, pp=pp: e.dma_start(
        out=trig[:, pp * 512:pp * 512 + 256],
        in_=ctx["coso"][:, slot * 256:(slot + 1) * 256]),
        waits=[(s_sw, s_sw.n)], incs=[I(s_trg[pp], 16)])
    sc.op("sync", lambda e, slot=slot, pp=pp: e.dma_start(
        out=trig[:, pp * 512 + 256:pp * 512 + 512],
        in_=ctx["sino"][:, slot * 256:(slot + 1) * 256]),
        incs=[I(s_trg[pp], 16)])
    trg_tok = (s_trg[pp], s_trg[pp].n)

    rsqrt_newton(sc, [A["ssq_tok" + str(pp)]], hop2,
                 qb[pp][0:1, 256:512],
                 rr[0:1, 0:256], rr[0:1, 256:512], rr[0:1, 512:768],
                 1.0 / DH, EPS_RMS, s_kc)
    sc.op("tensor", lambda e, oc=oc, pp=pp: e.matmul(
        qb[pp][:, 0:256], ctx["c_gq"][0:1, oc * 128:(oc + 1) * 128],
        rr[0:1, 512:768], start=True, stop=True),
        waits=[(s_kc, s_kc.n)], incs=[I(s_pe)])
    sc.op("vector", lambda e, pp=pp: e.tensor_copy(bc, qb[pp][:, 0:256]),
          waits=[(s_pe, s_pe.n)], incs=[I(s_kc)])
    sc.op("vector", lambda e, kf=kf: e.tensor_tensor(kh, kf, bc, ALU.mult),
          waits=[(s_sw, s_sw.n)], incs=[I(s_kc)])
    sc.op("gpsimd", lambda e: e.dma_start(out=ksw[0:64, :],
                                          in_=kh[64:128, :]),
          waits=[(s_kc, s_kc.n)], incs=[I(s_sw, 16)])
    sc.op("gpsimd", lambda e: e.dma_start(out=ksw[64:128, :],
                                          in_=kh[0:64, :]),
          incs=[I(s_sw, 16)])

    def rope(e, pp=pp):
        e.tensor_tensor(t1, kh, trig[:, pp * 512:pp * 512 + 256], ALU.mult)
        e.tensor_tensor(krot, ksw, trig[:, pp * 512 + 256:pp * 512 + 512],
                        ALU.mult)
        e.drain()
        return e.tensor_tensor(krot, krot, t1, ALU.add)
    sc.op("vector", rope, waits=[(s_sw, s_sw.n), trg_tok], incs=[I(s_kc)])

    def split(e, pp=pp):
        e.tensor_copy(qvo[:, pp * 512:pp * 512 + 256], krot)
        e.drain()
        return e.tensor_tensor(qvo[:, pp * 512 + 256:pp * 512 + 512], krot,
                               qvo[:, pp * 512:pp * 512 + 256], ALU.subtract)
    w = [A["sto_tok"].get(idx - 2)]
    sc.op("vector", split, waits=w, incs=[I(s_stg)])
    sc.op("gpsimd", lambda e, oc=oc, slot=slot, pp=pp: e.dma_start(
        out=ctx["q_scr"][slot, oc, :, :],
        in_=qvo[:, pp * 512:pp * 512 + 512]),
        waits=[(s_stg, s_stg.n)], incs=[I(s_sto[pp], 16)])
    A["sto_tok"][idx] = (s_sto[pp], s_sto[pp].n)



def build_b(nc, sc, ctx):
    I = sc.inc
    st = ExitStack()

    def sbl(n, s, d):
        return st.enter_context(nc.sbuf_tensor(n, s, d))

    def psl(n, s, d):
        return st.enter_context(nc.psum_tensor(n, s, d))

    kvg = sbl("b_kvg", [128, 2 * NKB * 512], BF16)
    kvo = sbl("b_kvo", [128, 2 * 8 * 512], BF16)
    qsb = sbl("b_qsb", [128, 2 * 4 * 512], BF16)
    p32 = sbl("b_p32", [128, 8 * 256], F32)
    pbh = sbl("b_pbh", [128, 8 * 256], BF16)
    pbl = sbl("b_pbl", [128, 8 * 256], BF16)
    rr = sbl("b_rr", [1, 4 * 256], F32)
    bcv = sbl("b_bcv", [128, 256], F32)
    onrm = sbl("b_onrm", [128, 256], F32)
    ost = sbl("b_ost", [128, 2 * 512], BF16)

    ps = [psl(f"b_ps{i}", [128, 512], F32) for i in range(4)]
    po = [psl(f"b_po{i}", [128, 512], F32) for i in range(2)]
    pl = [psl(f"b_pl{i}", [128, 512], F32) for i in range(2)]

    s_kvl = [Sem(nc, "b_kvl0"), Sem(nc, "b_kvl1")]
    s_kol = [Sem(nc, "b_kol0"), Sem(nc, "b_kol1")]
    s_ql = [Sem(nc, "b_ql0"), Sem(nc, "b_ql1")]
    s_sc = Sem(nc, "b_sc")
    s_cp = Sem(nc, "b_cp")
    s_mk = Sem(nc, "b_mk")
    s_ex = Sem(nc, "b_ex")
    s_sp = Sem(nc, "b_sp")
    s_pv = Sem(nc, "b_pv")
    s_tl = Sem(nc, "b_tl")
    s_pe2 = Sem(nc, "b_pe2")
    s_sto = Sem(nc, "b_sto")

    B = dict(kvg=kvg, kvo=kvo, qsb=qsb, p32=p32, pbh=pbh, pbl=pbl,
             rr=rr, bcv=bcv,
             onrm=onrm, ost=ost, ps=ps, po=po, pl=pl, s_kvl=s_kvl,
             s_ql=s_ql, s_sc=s_sc, s_cp=s_cp, s_mk=s_mk, s_ex=s_ex,
             s_sp=s_sp,
             s_pv=s_pv, s_tl=s_tl, s_pe2=s_pe2, s_sto=s_sto)

    KVSEG = 8
    s_cc = ctx["s_cc"]

    def load_kv(g, buf, gate):
        for k in range(KVSEG):
            nblk = NKB // KVSEG
            w = [(s_cc, g + 1)]
            if k == 0 and gate:
                w.append(gate)
            for bb in range(2):
                sc.op("sync", lambda e, g=g, k=k, buf=buf, nblk=nblk, bb=bb:
                      e.dma_start(
                          out=kvg[:, buf * NKB * 512 + k * nblk * 512:
                                  buf * NKB * 512 + (k + 1) * nblk * 512]
                          .rearrange("p (q f) -> p q f", f=512)[:, bb::2, :],
                          in_=ctx["kv_gath"][g, 4 * (k % 2):4 * (k % 2) + 4,
                                             k // 2, bb, :, :]
                          .rearrange("c p f -> p c f")), waits=w,
                      incs=[I(s_kvl[buf], 16)])
                w = []
        return (s_kvl[buf], s_kvl[buf].n)

    def load_kvo(g, buf, gate):
        w = [(s_cc, 1)]
        if gate:
            w.append(gate)
        sc.op("sync", lambda e, g=g, buf=buf: e.dma_start(
            out=kvo[:, buf * 4096:(buf + 1) * 4096]
            .rearrange("p (j f) -> p j f", j=8),
            in_=ctx["kv_own"][g, :, :, :].rearrange("j p f -> p j f")),
            waits=w, incs=[I(s_kol[buf], 16)])
        return (s_kol[buf], s_kol[buf].n)

    B["kv_ld"] = [load_kv(0, 0, None)]
    B["kvo_ld"] = [load_kvo(0, 0, None)]
    qfree = {}
    pv_end_of_g = []
    for g in range(G):
        gb = g % 2
        for slot in range(NSLOT):
            sidx = g * NSLOT + slot
            qpp = sidx % 2
            w = [qfree.get(sidx - 2)]
            for h in range(4):
                sc.op("sync", lambda e, g=g, slot=slot, h=h, qpp=qpp:
                      e.dma_start(
                          out=qsb[:, qpp * 2048 + h * 512:
                                  qpp * 2048 + (h + 1) * 512],
                          in_=ctx["q_scr"][slot, g * 4 + h, :, :]),
                      waits=w if h == 0 else [], incs=[I(s_ql[qpp], 16)])
            B["ql_tok"] = (s_ql[qpp], s_ql[qpp].n)
            for wv in range(2):
                B["pend_tail"] = _b_wave(nc, sc, ctx, B, g, slot, wv)
            qfree[sidx] = (s_pv, s_pv.n)
        pv_end_of_g.append((s_pv, s_pv.n))
        if g + 1 < G:
            gate = pv_end_of_g[g - 1] if g >= 1 else None
            B["kv_ld"].append(load_kv(g + 1, 1 - gb, gate))
            B["kvo_ld"].append(load_kvo(g + 1, 1 - gb, gate))

    if B.get("pend_tail"):
        B["pend_tail"][0]()
        B["pend_tail"][1]()
    sc.op("sync", lambda e: e.nop(), waits=[(s_sto, s_sto.n)])
    sc.emit("B")
    st.close()


def _b_wave(nc, sc, ctx, B, g, slot, wv):
    I = sc.inc
    Kj = KS[slot]
    KjT = Kj + 2
    gb = g % 2
    sidx = g * NSLOT + slot
    qpp = sidx % 2
    kvg, qsb, p32, pbh, pbl = B["kvg"], B["qsb"], B["p32"], B["pbh"], B["pbl"]
    rr, bcv, onrm, ost = B["rr"], B["bcv"], B["onrm"], B["ost"]
    ps, po, pl = B["ps"], B["po"], B["pl"]
    s_kvl, s_ql, s_sc, s_mk, s_ex, s_sp, s_pv, s_tl, s_pe2, s_sto = (
        B["s_kvl"], B["s_ql"], B["s_sc"], B["s_mk"], B["s_ex"], B["s_sp"],
        B["s_pv"], B["s_tl"], B["s_pe2"], B["s_sto"])
    s_cp = B["s_cp"]
    kv_ld = B["kv_ld"][g]
    kvo_ld = B["kvo_ld"][g]
    ql_tok = B["ql_tok"]
    assert isinstance(kv_ld, tuple) and isinstance(ql_tok, tuple)

    def kofs(it, which):
        off = {"kh": 0, "kl": 128, "vh": 256, "vl": 384}[which]
        if it < Kj:
            base = gb * (NKB * 512) + it * 512
            return kvg[:, base + off:base + off + 128]
        base = gb * 4096 + (6 - 2 * slot + (it - Kj)) * 512
        return B["kvo"][:, base + off:base + off + 128]

    def qofs(h, part):
        base = qpp * 2048 + h * 512 + (0 if part == "h" else 256)
        return qsb[:, base:base + 256]

    heads = (2 * wv, 2 * wv + 1)
    exp_tok = {}
    pv_tok = {}
    split_tok = {}
    exp_prev = (s_ex, B.get("exp_end", 0))
    if B.get("pend_tail"):
        B["pend_tail"][0]()

    def emit_pv(it):
        ib = it % 2

        def pvmm(e, it=it, Kj=KjT):
            r = None
            for hi in range(2):
                sl = slice((hi * 4 + it % 4) * 256,
                           (hi * 4 + it % 4 + 1) * 256)
                e.matmul(po[hi][:, 0:256], kofs(it, "vh"), pbh[:, sl],
                         start=(it == 0), stop=False)
                e.matmul(po[hi][:, 0:256], kofs(it, "vl"), pbh[:, sl],
                         start=False, stop=False)
                e.matmul(po[hi][:, 0:256], kofs(it, "vh"), pbl[:, sl],
                         start=False, stop=(it == Kj - 1))
                e.matmul(pl[hi][0:1, 0:256], ctx["c_ones"][:, :], pbh[:, sl],
                         start=(it == 0), stop=False)
                r = e.matmul(pl[hi][0:1, 0:256], ctx["c_ones"][:, :],
                             pbl[:, sl], start=False, stop=(it == Kj - 1))
            return r
        w = [(s_sp, split_tok[it])]
        if it == 0 and B.get("tail_tok"):
            w.append(B["tail_tok"])
        sc.op("tensor", pvmm, waits=w, incs=[I(s_pv)])
        pv_tok[it] = s_pv.n

    for it in range(KjT):
        ib = it % 2

        def smm(e, it=it, ib=ib):
            r = None
            for hi, h in enumerate(heads):
                bank = ps[hi * 2 + ib]
                e.matmul(bank[:, 0:256], kofs(it, "kh"), qofs(h, "h"),
                         start=True, stop=False)
                e.matmul(bank[:, 0:256], kofs(it, "kl"), qofs(h, "h"),
                         start=False, stop=False)
                r = e.matmul(bank[:, 0:256], kofs(it, "kh"), qofs(h, "l"),
                             start=False, stop=True)
            return r
        w = [kv_ld, ql_tok]
        if it >= Kj:
            w.append(kvo_ld)
        if it >= 2:
            w.append((s_ex, exp_tok[it - 2]))
        else:
            w.append(exp_prev)
        sc.op("tensor", smm, waits=w, incs=[I(s_sc)])

        diag = it >= Kj
        if diag:
            mask = ctx["c_maskA"] if it == Kj else ctx["c_maskB"]

            def madd(e, ib=ib, mask=mask):
                r = None
                for hi in range(2):
                    bank = ps[hi * 2 + ib]
                    r = e.tensor_tensor(bank[:, 0:256], bank[:, 0:256],
                                        mask[:, :], ALU.add)
                return r
            sc.op("vector", madd, waits=[(s_sc, s_sc.n)], incs=[I(s_mk)])

        def expf(e, it=it, ib=ib, slot=slot):
            r = None
            for hi in range(2):
                bank = ps[hi * 2 + ib]
                sl4 = (hi * 4 + it % 4) * 256
                r = e.activation(
                    p32[:, sl4:sl4 + 256],
                    bank[:, 0:256], AF.Exp,
                    bias=ctx["c_bias"][:, slot * (NKB + 2) + it:
                                       slot * (NKB + 2) + it + 1],
                    scale=DH ** -0.5)
            return r
        w = [(s_sc, s_sc.n)]
        if diag:
            w.append((s_mk, s_mk.n))
        if it >= 4:
            w.append((s_sp, split_tok[it - 4]))
        sc.op("scalar", expf, waits=w, incs=[I(s_ex)])
        exp_tok[it] = s_ex.n

        def cph(e, it=it):
            r = None
            for hi in range(2):
                sl = slice((hi * 4 + it % 4) * 256,
                           (hi * 4 + it % 4 + 1) * 256)
                r = e.activation(pbh[:, sl], p32[:, sl], AF.Copy,
                                 bias=0.0, scale=1.0)
            return r
        w = [(s_ex, s_ex.n)]
        if it >= 4:
            w.append((s_pv, pv_tok[it - 4]))
        sc.op("scalar", cph, waits=w, incs=[I(s_cp)])

        def spl2(e, it=it):
            r = None
            for hi in range(2):
                sl = slice((hi * 4 + it % 4) * 256,
                           (hi * 4 + it % 4 + 1) * 256)
                r = e.tensor_tensor(pbl[:, sl], p32[:, sl], pbh[:, sl],
                                    ALU.subtract)
            return r
        w = [(s_cp, s_cp.n)]
        if it >= 4:
            w.append((s_pv, pv_tok[it - 4]))
        sc.op("vector", spl2, waits=w, incs=[I(s_sp)])
        split_tok[it] = s_sp.n
        if it == 1 and B.get("pend_tail"):
            B["pend_tail"][1]()
            B["pend_tail"] = None
            B["tail_tok"] = B["po_free"]
        if it >= 3:
            emit_pv(it - 3)
    emit_pv(KjT - 3)
    emit_pv(KjT - 2)
    emit_pv(KjT - 1)

    B["exp_end"] = s_ex.n

    def tail_stage1(pv_end=(s_pv, s_pv.n)):
        def recips(e):
            e.reciprocal(rr[0:1, 0:256], pl[0][0:1, 0:256])
            return e.reciprocal(rr[0:1, 512:768], pl[1][0:1, 0:256])
        sc.op("vector", recips, waits=[pv_end], incs=[I(s_tl)])
        B["rc_tok"] = (s_tl, s_tl.n)

    def tail_stage2(heads=heads, g=g, slot=slot):
        rc_tok = B["rc_tok"]
        for hi in range(2):
            sc.op("tensor", lambda e, hi=hi: e.matmul(
                po[hi][:, 256:512], ctx["c_onesrf"][:, :],
                rr[0:1, hi * 512:hi * 512 + 256],
                start=True, stop=True), waits=[rc_tok], incs=[I(s_pe2)])
        onrm_q = []
        for hi, h in enumerate(heads):
            oc = g * 4 + h
            sc.op("vector", lambda e, hi=hi: e.tensor_copy(
                bcv[:, :], po[hi][:, 256:512]),
                waits=[(s_pe2, s_pe2.n - (1 - hi))], incs=[I(s_tl)])
            sc.op("vector", lambda e, hi=hi: e.tensor_tensor(
                onrm[:, :], po[hi][:, 0:256], bcv[:, :], ALU.mult),
                incs=[I(s_tl)])
            B["po_free"] = (s_tl, s_tl.n)

            def osplit(e, hi=hi):
                e.tensor_copy(ost[:, hi * 512:hi * 512 + 256], onrm[:, :])
                e.drain()
                return e.tensor_tensor(ost[:, hi * 512 + 256:hi * 512 + 512],
                                       onrm[:, :],
                                       ost[:, hi * 512:hi * 512 + 256],
                                       ALU.subtract)
            sc.op("vector", osplit, waits=[(s_sto, s_sto.n)], incs=[I(s_tl)])
            sc.op("gpsimd", lambda e, oc=oc, slot=slot, hi=hi: e.dma_start(
                out=ctx["o_scr"][oc, slot, :, :],
                in_=ost[:, hi * 512:hi * 512 + 512]),
                waits=[(s_tl, s_tl.n)], incs=[I(s_sto, 16)])
    return (tail_stage1, tail_stage2)


def build_c(nc, sc, ctx):
    I = sc.inc
    st = ExitStack()

    def sbl(n, s, d):
        return st.enter_context(nc.sbuf_tensor(n, s, d))

    def psl(n, s, d):
        return st.enter_context(nc.psum_tensor(n, s, d))

    woh = sbl("c_woh", [128, HC * 1024], BF16)
    wol = sbl("c_wol", [128, HC * 1024], BF16)
    wldf = sbl("c_wldf", [128, 2 * 1024], F32)
    osb = sbl("c_osb", [128, 2 * HC * 512], BF16)
    xov = sbl("c_xov", [128, 3 * 256], F32)
    x2f = sbl("c_x2f", [128, 3 * 256], F32)
    x2p = sbl("c_x2p", [128, 3 * 256], F32)

    pc = [psl(f"c_pc{i}", [128, 512], F32) for i in range(3)]
    pT = [psl(f"c_pT{i}", [128, 512], F32) for i in range(3)]

    s_ldw = [Sem(nc, "c_ldw0"), Sem(nc, "c_ldw1")]
    s_wsp = Sem(nc, "c_wsp")
    s_ol = [Sem(nc, "c_ol0"), Sem(nc, "c_ol1")]
    s_xl = [Sem(nc, f"c_xl{i}") for i in range(3)]
    s_mm = Sem(nc, "c_mm")
    s_ev = Sem(nc, "c_ev")
    s_tr = Sem(nc, "c_tr")
    s_ad = Sem(nc, "c_ad")
    s_sto = [Sem(nc, f"c_sto{i}") for i in range(3)]

    ofree = {}
    idx = 0
    ev_tokd = {}
    tr_tokd = {}
    ad_tokd = {}
    sto_tokd = {}
    pend = []

    def emit_cpost(idx, oc, slot, epp):
        def trp(e, epp=epp):
            r = None
            for b in range(2):
                r = e.transpose(pT[epp][:, b * 128:(b + 1) * 128],
                                x2f[:, epp * 256 + b * 128:
                                    epp * 256 + (b + 1) * 128],
                                ctx["c_idf"][:, :])
            return r
        w = [(s_ev, ev_tokd[idx])]
        if idx >= 3:
            w.append((s_ad, ad_tokd[idx - 3]))
        sc.op("tensor", trp, waits=w, incs=[I(s_tr)])
        tr_tokd[idx] = s_tr.n

        for b in range(2):
            tl = slot * 2 + b
            w = []
            if b == 0 and idx >= 3:
                w.append((s_ad, ad_tokd[idx - 3]))
            sc.op("sync", lambda e, tl=tl, oc=oc, epp=epp, b=b:
                  e.dma_start(
                      out=xov[:, epp * 256 + b * 128:
                              epp * 256 + (b + 1) * 128],
                      in_=ctx["xown"][tl, :, oc * 128:(oc + 1) * 128]),
                  waits=w, incs=[I(s_xl[epp], 16)])

        def radd(e, epp=epp):
            r = None
            for b in range(2):
                r = e.tensor_tensor(
                    x2p[:, epp * 256 + b * 128:epp * 256 + (b + 1) * 128],
                    pT[epp][:, b * 128:(b + 1) * 128],
                    xov[:, epp * 256 + b * 128:epp * 256 + (b + 1) * 128],
                    ALU.add)
            return r
        w = [(s_tr, tr_tokd[idx]), (s_xl[epp], s_xl[epp].n)]
        if idx >= 3:
            w.append(sto_tokd[idx - 3])
        sc.op("vector", radd, waits=w, incs=[I(s_ad)])
        ad_tokd[idx] = s_ad.n
        for b in range(2):
            tl = slot * 2 + b
            sc.op("gpsimd", lambda e, tl=tl, oc=oc, epp=epp, b=b:
                  e.dma_start(
                      out=ctx["x2_scr"][tl, :, oc * 128:(oc + 1) * 128],
                      in_=x2p[:, epp * 256 + b * 128:
                              epp * 256 + (b + 1) * 128]),
                  waits=[(s_ad, s_ad.n)] if b == 0 else [],
                  incs=[I(s_sto[epp], 16)])
        sto_tokd[idx] = (s_sto[epp], s_sto[epp].n)

    for half in range(2):
        wldf_free = {}
        half_gate = (s_mm, s_mm.n) if half else None
        for dc in range(HC):
            b = dc % 2
            w = [wldf_free.get(dc - 2)]
            if half_gate:
                w.append(half_gate)
            sc.op("sync", lambda e, dc=dc, b=b, half=half: e.dma_start(
                out=wldf[:, b * 1024:(b + 1) * 1024],
                in_=ctx["wo"][dc * 128:(dc + 1) * 128,
                              half * 1024:(half + 1) * 1024]), waits=w,
                incs=[I(s_ldw[b], 16)])
            sc.op("vector", lambda e, dc=dc, b=b: e.tensor_copy(
                woh[:, dc * 1024:(dc + 1) * 1024],
                wldf[:, b * 1024:(b + 1) * 1024]),
                waits=[(s_ldw[b], s_ldw[b].n)], incs=[I(s_wsp)])
            sc.op("vector", lambda e, dc=dc, b=b: e.tensor_tensor(
                wol[:, dc * 1024:(dc + 1) * 1024],
                wldf[:, b * 1024:(b + 1) * 1024],
                woh[:, dc * 1024:(dc + 1) * 1024], ALU.subtract),
                incs=[I(s_wsp)])
            wldf_free[dc] = (s_wsp, s_wsp.n)
        wsp_half = s_wsp.n

        for slot in range(NSLOT):
            sl_i = half * NSLOT + slot
            spp = slot % 2
            w = [ofree.get(sl_i - 2)]
            for dc in range(HC):
                sc.op("sync", lambda e, slot=slot, dc=dc, spp=spp:
                      e.dma_start(
                          out=osb[:, spp * HC * 512 + dc * 512:
                                  spp * HC * 512 + (dc + 1) * 512],
                          in_=ctx["o_scr"][dc, slot, :, :]),
                      waits=w if dc == 0 else [], incs=[I(s_ol[spp], 16)])
            ol_tok = (s_ol[spp], s_ol[spp].n)
            for oci in range(8):
                oc = half * 8 + oci
                bank = pc[idx % 3]
                epp = idx % 3

                def cmm(e, oci=oci, spp=spp, bank=bank):
                    r = None
                    for dc in range(HC):
                        ob = spp * HC * 512 + dc * 512
                        ws = slice(dc * 1024 + oci * 128,
                                   dc * 1024 + (oci + 1) * 128)
                        e.matmul(bank[:, 0:256], woh[:, ws],
                                 osb[:, ob:ob + 256], start=(dc == 0),
                                 stop=False)
                        e.matmul(bank[:, 0:256], wol[:, ws],
                                 osb[:, ob:ob + 256], start=False,
                                 stop=False)
                        r = e.matmul(bank[:, 0:256], woh[:, ws],
                                     osb[:, ob + 256:ob + 512], start=False,
                                     stop=(dc == HC - 1))
                    return r
                w = [ol_tok, (s_wsp, wsp_half)]
                if idx >= 3:
                    w.append((s_ev, ev_tokd[idx - 3]))
                sc.op("tensor", cmm, waits=w, incs=[I(s_mm)])
                mm_tok = s_mm.n

                w = [(s_mm, mm_tok)]
                if idx >= 3:
                    w.append((s_tr, tr_tokd[idx - 3]))
                sc.op("scalar", lambda e, epp=epp, bank=bank: e.activation(
                    x2f[:, epp * 256:(epp + 1) * 256], bank[:, 0:256],
                    AF.Copy, bias=0.0, scale=1.0), waits=w, incs=[I(s_ev)])
                ev_tokd[idx] = s_ev.n
                pend.append((idx, oc, slot, epp))
                if len(pend) >= 2:
                    emit_cpost(*pend.pop(0))
                idx += 1
            ofree[sl_i] = (s_mm, s_mm.n)
    while pend:
        emit_cpost(*pend.pop(0))

    sc.op("sync", lambda e: e.nop(),
          waits=[(s_sto[0], s_sto[0].n), (s_sto[1], s_sto[1].n),
                 (s_sto[2], s_sto[2].n)])
    sc.emit("C")
    st.close()


def build_d(nc, sc, ctx):
    I = sc.inc
    st = ExitStack()

    def sbl(n, s, d):
        return st.enter_context(nc.sbuf_tensor(n, s, d))

    def psl(n, s, d):
        return st.enter_context(nc.psum_tensor(n, s, d))

    NH2 = NIC // 2
    hh = sbl("d_hh", [128, NH2 * 512], BF16)
    hl = sbl("d_hl", [128, NH2 * 512], BF16)
    dpart = sbl("d_dpart", [128, HC * 512], F32)
    x2h = sbl("d_x2h", [128, HC * 512], BF16)
    x2l = sbl("d_x2l", [128, HC * 512], BF16)
    x2t = None
    bns = sbl("d_bns", [128, 24], F32)
    mv = sbl("d_mv", [128, 2], F32)
    stt = sbl("d_stt", [128, 4], F32)
    lnc = sbl("d_lnc", [128, D], F32)
    lnh = sbl("d_lnh", [128, D], BF16)
    lnl = sbl("d_lnl", [128, D], BF16)
    wbuf = sbl("d_wbuf", [128, 16384], BF16)
    hrow = sbl("d_hrow", [1, 2 * 512], BF16)
    hrf = sbl("d_hrf", [1, 512], F32)
    gsl = sbl("d_gsl", [128, 2 * 512], F32)
    h32 = sbl("d_h32", [128, 512], F32)
    dev = gsl[:, 0:512]
    otm = lnc[:, 0:1024]

    pgu = [psl(f"d_pgu{i}", [128, 512], F32) for i in range(4)]
    pwT = [psl(f"d_pwT{i}", [128, 1024], BF16) for i in range(3)]
    pH = psl("d_pH", [128, 512], F32)
    pdn = pgu

    s_x2 = Sem(nc, "d_x2")
    hop = Sem(nc, "d_hop")
    s_ln = Sem(nc, "d_ln")
    s_sp = Sem(nc, "d_sp")
    s_xT = Sem(nc, "d_xT")
    s_xe = Sem(nc, "d_xe")
    s_wl = [Sem(nc, "d_wl0"), Sem(nc, "d_wl1")]
    s_wT = Sem(nc, "d_wT")
    s_we = Sem(nc, "d_we")
    s_gu = Sem(nc, "d_gu")
    s_si = Sem(nc, "d_si")
    s_hs = Sem(nc, "d_hs")
    s_H = Sem(nc, "d_H")
    s_dn = Sem(nc, "d_dn")
    s_de = Sem(nc, "d_de")
    s_de2 = Sem(nc, "d_de2")
    s_oT = Sem(nc, "d_oT")
    s_oa = Sem(nc, "d_oa")
    s_sto = Sem(nc, "d_sto")
    s_x2r = Sem(nc, "d_x2r")

    tbank = {"i": 0}
    lastev = {}

    def next_tb():
        b = tbank["i"] % 3
        tbank["i"] += 1
        return b

    sc.op("scalar", lambda e: e.nop(), waits=[ctx["scl_tok"]])

    W_DS, W_DT = 0, 4 * D

    pdn_free = {}
    si_free = {}
    pair_gate = None
    pair_gate2 = None
    for pair in range(2):
        xT_tok_prev = None
        for tl in range(4):
            gtl = pair * 4 + tl
            w = [pair_gate] if (pair and tl == 0) else []
            if tl >= 1:
                w = [(s_sp, s_sp.n)]
            w = []
            if pair and tl == 0:
                w.append(pair_gate)
                w.append(pair_gate2)
            if tl >= 1 or pair:
                w.append((s_ln, s_ln.n))
            sc.op("sync", lambda e, gtl=gtl: e.dma_start(
                out=dpart[:, 0:D], in_=ctx["x2_scr"][gtl, :, :]), waits=w,
                incs=[I(s_x2, 16)])
            ap_w = [(s_sp, s_sp.n)]
            ln_tile(sc, nc, dpart[:, 0:D], bns[:, :], mv[:, :], stt[:, :],
                    lnc[:, :], EPS_LN, [(s_x2, s_x2.n)], ap_w, [I(s_ln)],
                    hop)

            def spl(e):
                e.tensor_copy(lnh[:, :], lnc[:, :])
                e.drain()
                return e.tensor_tensor(lnl[:, :], lnc[:, :], lnh[:, :],
                                       ALU.subtract)
            w = [(s_ln, s_ln.n)]
            if xT_tok_prev is not None:
                w.append(xT_tok_prev)
            sc.op("vector", spl, waits=w, incs=[I(s_sp)])
            for r in range(8):
                tb = next_tb()

                def trx(e, r=r, tb=tb):
                    rr_ = None
                    for k in range(2):
                        hc = 2 * r + k
                        rr_ = e.transpose(
                            pwT[tb][:, k * 256:k * 256 + 128],
                            lnh[:, hc * 128:(hc + 1) * 128],
                            ctx["c_idb"][:, :])
                        rr_ = e.transpose(
                            pwT[tb][:, k * 256 + 128:k * 256 + 256],
                            lnl[:, hc * 128:(hc + 1) * 128],
                            ctx["c_idb"][:, :])
                    return rr_
                w = [(s_sp, s_sp.n), lastev.get(tb)]
                sc.op("tensor", trx, waits=w, incs=[I(s_xT)])

                def xev(e, r=r, tl=tl, tb=tb):
                    rr_ = None
                    for k in range(2):
                        hc = 2 * r + k
                        dst = hc * 512 + tl * 128
                        rr_ = e.tensor_copy(
                            x2h[:, dst:dst + 128],
                            pwT[tb][:, k * 256:k * 256 + 128])
                        rr_ = e.tensor_copy(
                            x2l[:, dst:dst + 128],
                            pwT[tb][:, k * 256 + 128:k * 256 + 256])
                    return rr_
                sc.op("vector", xev, waits=[(s_xT, s_xT.n)], incs=[I(s_xe)])
                lastev[tb] = (s_xe, s_xe.n)
            xT_tok_prev = (s_xT, s_xT.n)
        xe_done = s_xe.n

        wst_free = {}
        we_toks = {}
        hacc_pend = []

        def flush_hacc():
            while hacc_pend:
                fn, tok = hacc_pend.pop(0)
                sc.op("tensor", fn, waits=[(s_hs, tok)], incs=[I(s_H)])

        def emit_wload(ic, boundary):
            W_GS = (ic % 2) * 2 * D
            w = [wst_free.get(ic - 2)]
            if boundary:
                w.append((s_wT, s_wT.n))
            sc.op("gpsimd", lambda e, ic=ic, W_GS=W_GS: e.dma_start(
                out=wbuf[:, W_GS:W_GS + D],
                in_=ctx["gw"][ic * 128:(ic + 1) * 128, :]), waits=w,
                incs=[I(s_wl[ic % 2], 16)])
            sc.op("gpsimd", lambda e, ic=ic, W_GS=W_GS: e.dma_start(
                out=wbuf[:, W_GS + D:W_GS + 2 * D],
                in_=ctx["uw"][ic * 128:(ic + 1) * 128, :]),
                incs=[I(s_wl[ic % 2], 16)])

        def emit_wtrans(ic):
            wl_tok = (s_wl[ic % 2], s_wl[ic % 2].n)
            W_GS = (ic % 2) * 2 * D
            GT = 4 * D + (ic % 2) * 2 * D
            for part in range(2):
                src_off = W_GS + part * D
                dst_off = GT + part * D
                for r in range(4):
                    tb = next_tb()

                    def trw(e, part=part, r=r, src_off=src_off, tb=tb):
                        rr_ = None
                        for k in range(4):
                            hc = 4 * r + k
                            rr_ = e.transpose(
                                pwT[tb][:, k * 128:(k + 1) * 128],
                                wbuf[:, src_off + hc * 128:
                                     src_off + (hc + 1) * 128],
                                ctx["c_idb"][:, :])
                        return rr_
                    w = [wl_tok, (s_xe, xe_done), lastev.get(tb)]
                    sc.op("tensor", trw, waits=w, incs=[I(s_wT)])
                    sc.op("vector", lambda e, r=r, dst_off=dst_off, tb=tb:
                          e.tensor_copy(
                              wbuf[:, dst_off + r * 512:
                                   dst_off + (r + 1) * 512],
                              pwT[tb][:, 0:512]),
                          waits=[(s_wT, s_wT.n)], incs=[I(s_we)])
                    lastev[tb] = (s_we, s_we.n)
            wst_free[ic] = (s_wT, s_wT.n)
            we_toks[ic] = (s_we, s_we.n)

        for icb in range(2):
            first = icb * NH2
            emit_wload(first, True)
            emit_wtrans(first)
            for ici in range(NH2):
                ic = icb * NH2 + ici
                if ici + 1 < NH2:
                    emit_wload(ic + 1, False)

                gp = 2 * (ic % 2)
                GT = 4 * D + (ic % 2) * 2 * D

                def gumm(e, gp=gp, GT=GT):
                    r = None
                    for part, bank in ((0, pgu[gp]), (1, pgu[gp + 1])):
                        woff = GT + part * D
                        for hc in range(HC):
                            ws = slice(woff + hc * 128,
                                       woff + (hc + 1) * 128)
                            e.matmul(bank[:, 0:512], wbuf[:, ws],
                                     x2h[:, hc * 512:(hc + 1) * 512],
                                     start=(hc == 0), stop=False)
                            r = e.matmul(bank[:, 0:512], wbuf[:, ws],
                                         x2l[:, hc * 512:(hc + 1) * 512],
                                         start=False, stop=(hc == HC - 1))
                    return r
                w = [(s_xe, xe_done), we_toks[ic]]
                w += si_free.get(ic - 2, [])
                if ici == 0:
                    w.append((s_dn, s_dn.n))
                sc.op("tensor", gumm, waits=w, incs=[I(s_gu)])
                gu_tok = s_gu.n
                if ici + 1 < NH2:
                    emit_wtrans(ic + 1)

                w = [(s_gu, gu_tok), (s_hs, s_hs.n)]
                sc.op("scalar", lambda e, ic=ic, gp=gp: e.activation(
                    gsl[:, 0:512], pgu[gp][:, 0:512], AF.Silu,
                    scale=ctx["c_sg"][:, ic:ic + 1]), waits=w,
                    incs=[I(s_si)])
                sc.op("scalar", lambda e, ic=ic, gp=gp: e.activation(
                    gsl[:, 512:1024], pgu[gp + 1][:, 0:512], AF.Identity,
                    scale=ctx["c_su"][:, ic:ic + 1]),
                    waits=[(s_gu, gu_tok)], incs=[I(s_si)])
                si_free[ic] = [(s_si, s_si.n)]

                def hmul(e, ic=ic, ici=ici):
                    e.tensor_tensor(h32[:, :], gsl[:, 0:512],
                                    gsl[:, 512:1024], ALU.mult)
                    e.drain()
                    e.tensor_copy(hh[:, ici * 512:(ici + 1) * 512],
                                  h32[:, :])
                    e.drain()
                    return e.tensor_tensor(hl[:, ici * 512:(ici + 1) * 512],
                                           h32[:, :],
                                           hh[:, ici * 512:(ici + 1) * 512],
                                           ALU.subtract)
                sc.op("vector", hmul, waits=[(s_si, s_si.n)], incs=[I(s_hs)])

                def hacc(e, ic=ic, ici=ici):
                    e.matmul(pH[0:1, 0:512], ctx["c_ones"][:, :],
                             hh[:, ici * 512:(ici + 1) * 512],
                             start=(ic == 0), stop=False)
                    return e.matmul(pH[0:1, 0:512], ctx["c_ones"][:, :],
                                    hl[:, ici * 512:(ici + 1) * 512],
                                    start=False, stop=(ic == NIC - 1))
                hacc_pend.append((hacc, s_hs.n))
                if len(hacc_pend) > 1:
                    fn, tok = hacc_pend.pop(0)
                    sc.op("tensor", fn, waits=[(s_hs, tok)], incs=[I(s_H)])
            flush_hacc()
            gu_half_done = s_hs.n

            if icb == 1:
                def hsplit(e):
                    e.tensor_copy(hrf[0:1, :], pH[0:1, 0:512])
                    e.drain()
                    e.tensor_copy(hrow[0:1, 0:512], hrf[0:1, :])
                    e.drain()
                    return e.tensor_tensor(hrow[0:1, 512:1024], hrf[0:1, :],
                                           hrow[0:1, 0:512], ALU.subtract)
                sc.op("vector", hsplit, waits=[(s_H, s_H.n)], incs=[I(s_hs)])
            hrow_tok = s_hs.n

            wl_toks = {}
            we_toks = {}

            def emit_dload(hc):
                w = [(s_wT, s_wT.n)]
                sc.op("gpsimd", lambda e, hc=hc, icb=icb: e.dma_start(
                    out=wbuf[:, W_DS:W_DS + NH2 * 128],
                    in_=ctx["dw"][hc * 128:(hc + 1) * 128,
                                  icb * (NH2 * 128):
                                  (icb + 1) * (NH2 * 128)]),
                    waits=w, incs=[I(s_wl[hc % 2], 16)])
                wl_toks[hc] = (s_wl[hc % 2], s_wl[hc % 2].n)

            def emit_trd(hc):
                for r in range(8):
                    tb = next_tb()

                    def trd(e, r=r, tb=tb):
                        rr_ = None
                        for k in range(4):
                            ici = 4 * r + k
                            rr_ = e.transpose(
                                pwT[tb][:, k * 128:(k + 1) * 128],
                                wbuf[:, W_DS + ici * 128:
                                     W_DS + (ici + 1) * 128],
                                ctx["c_idb"][:, :])
                        return rr_
                    w = [wl_toks[hc], lastev.get(tb)]
                    sc.op("tensor", trd, waits=w, incs=[I(s_wT)])
                    sc.op("vector", lambda e, r=r, tb=tb: e.tensor_copy(
                        wbuf[:, W_DT + r * 512:W_DT + (r + 1) * 512],
                        pwT[tb][:, 0:512]),
                        waits=[(s_wT, s_wT.n)], incs=[I(s_we)])
                    lastev[tb] = (s_we, s_we.n)
                we_toks[hc] = s_we.n

            def emit_dmm(hc):
                dpp = hc % 2

                def dmm(e, hc=hc, dpp=dpp, icb=icb):
                    bank = pdn[dpp]
                    first = True
                    if icb == 1:
                        e.matmul(bank[:, 0:512], ctx["c_n128"][:, :],
                                 hrow[0:1, 0:512], start=True, stop=False)
                        e.matmul(bank[:, 0:512], ctx["c_n128"][:, :],
                                 hrow[0:1, 512:1024], start=False,
                                 stop=False)
                        first = False
                    r = None
                    for ici in range(NH2):
                        ws = slice(W_DT + ici * 128, W_DT + (ici + 1) * 128)
                        e.matmul(bank[:, 0:512], wbuf[:, ws],
                                 hh[:, ici * 512:(ici + 1) * 512],
                                 start=first, stop=False)
                        first = False
                        r = e.matmul(bank[:, 0:512], wbuf[:, ws],
                                     hl[:, ici * 512:(ici + 1) * 512],
                                     start=False, stop=(ici == NH2 - 1))
                    return r
                ghc = (pair * 2 + icb) * HC + hc
                w = [(s_we, we_toks[hc]), (s_hs, hrow_tok)]
                w += pdn_free.get(ghc - 2, [])
                sc.op("tensor", dmm, waits=w, incs=[I(s_dn)])
                dn_toks[hc] = s_dn.n

            def emit_stash(hc):
                dpp = hc % 2
                ghc = (pair * 2 + icb) * HC + hc
                sc.op("vector", lambda e, hc=hc, dpp=dpp: e.tensor_copy(
                    dpart[:, hc * 512:(hc + 1) * 512],
                    pdn[dpp][:, 0:512]), waits=[(s_dn, dn_toks[hc])],
                    incs=[I(s_de)])
                pdn_free[ghc] = [(s_de, s_de.n)]

            def emit_tail(hc):
                dpp = hc % 2
                ghc = (pair * 2 + icb) * HC + hc
                w = [(s_dn, dn_toks[hc])]
                if s_oT.n >= 1:
                    w.append((s_oT, s_oT.n))
                sc.op("vector", lambda e, hc=hc, dpp=dpp: e.tensor_tensor(
                    dev[:, :], pdn[dpp][:, 0:512],
                    dpart[:, hc * 512:(hc + 1) * 512], ALU.add), waits=w,
                    incs=[I(s_de)])
                pdn_free[ghc] = [(s_de, s_de.n)]
                sc.op("scalar", lambda e, hc=hc: e.activation(
                    dev[:, :], dev[:, :], AF.Copy, bias=0.0,
                    scale=ctx["c_sd"][:, hc:hc + 1]),
                    waits=[(s_de, s_de.n)], incs=[I(s_de2)])
                de_tok = s_de2.n

                def dtr(e, hc=hc):
                    rr_ = None
                    for b in range(4):
                        rr_ = e.transpose(
                            pgu[2 + b % 2][:, (b // 2) * 128:
                                           (b // 2) * 128 + 128],
                            dev[:, b * 128:(b + 1) * 128],
                            ctx["c_idf"][:, :])
                    return rr_
                w = [(s_de2, de_tok)]
                if s_oT.n >= 1:
                    w.append((s_oa, s_oa.n))
                sc.op("tensor", dtr, waits=w, incs=[I(s_oT)])
                for b in range(4):
                    gtl = pair * 4 + b
                    w = [(s_oa, s_oa.n), (s_sp, s_sp.n)] if b == 0 else []
                    sc.op("sync", lambda e, gtl=gtl, hc=hc, b=b: e.dma_start(
                        out=otm[:, 512 + b * 128:512 + (b + 1) * 128],
                        in_=ctx["x2_scr"][gtl, :, hc * 128:(hc + 1) * 128]),
                        waits=w, incs=[I(s_x2r, 16)])

                def oadd(e, hc=hc):
                    rr_ = None
                    for b in range(4):
                        rr_ = e.tensor_tensor(
                            otm[:, b * 128:(b + 1) * 128],
                            pgu[2 + b % 2][:, (b // 2) * 128:
                                           (b // 2) * 128 + 128],
                            otm[:, 512 + b * 128:512 + (b + 1) * 128],
                            ALU.add)
                    return rr_
                sc.op("vector", oadd,
                      waits=[(s_oT, s_oT.n), (s_x2r, s_x2r.n),
                             (s_sto, s_sto.n)], incs=[I(s_oa)])
                for b in range(4):
                    gtl = pair * 4 + b
                    sc.op("gpsimd", lambda e, gtl=gtl, hc=hc, b=b:
                          e.dma_start(
                              out=ctx["out"][gtl, :,
                                             hc * 128:(hc + 1) * 128],
                              in_=otm[:, b * 128:(b + 1) * 128]),
                          waits=[(s_oa, s_oa.n)] if b == 0 else [],
                          incs=[I(s_sto, 16)])

            dn_toks = {}
            emit_dload(0)
            emit_trd(0)
            tail_hc = None
            for hc in range(HC):
                if hc + 1 < HC:
                    emit_dload(hc + 1)
                emit_dmm(hc)
                if hc + 1 < HC:
                    emit_trd(hc + 1)
                if icb == 0:
                    emit_stash(hc)
                else:
                    if tail_hc is not None:
                        emit_tail(tail_hc)
                    tail_hc = hc
            if icb == 1:
                emit_tail(tail_hc)

        pair_gate = (s_oa, s_oa.n)
        pair_gate2 = (s_sto, s_sto.n)

    sc.op("sync", lambda e: e.nop(), waits=[(s_sto, s_sto.n)])
    sc.op("gpsimd", lambda e: e.nop(), waits=[(s_sto, s_sto.n)])
    sc.emit("D")
    st.close()


_NC_CACHE = {}


def _get_nc(dbg=DBG, phases=("A1", "A2", "B", "C", "D")):
    key = (dbg, tuple(phases))
    if key not in _NC_CACHE:
        _NC_CACHE[key] = build_nc(dbg, phases)
    return _NC_CACHE[key]


def make_inmaps(inputs):
    cos_t, sin_t, maskA, maskB = host_tables()
    id_f = np.eye(128, dtype=np.float32)
    id_b = np.eye(128, dtype=BFNP)
    x = np.ascontiguousarray(
        np.asarray(inputs["L0_attn_x"], dtype=np.float32).reshape(T, D))
    gq = np.ascontiguousarray(
        np.asarray(inputs["L0_attn_gq"], np.float32).reshape(-1))
    gk = np.asarray(inputs["L0_attn_gk"], np.float32)
    gk_rep = np.ascontiguousarray(np.repeat(gk, 4, axis=0).reshape(-1))
    shared = {
        "L0_attn_wq": np.ascontiguousarray(
            np.asarray(inputs["L0_attn_wq"], np.float32)),
        "L0_attn_wk": np.ascontiguousarray(
            np.asarray(inputs["L0_attn_wk"], np.float32)),
        "L0_attn_wv": np.ascontiguousarray(
            np.asarray(inputs["L0_attn_wv"], np.float32)),
        "L0_attn_wo": np.ascontiguousarray(
            np.asarray(inputs["L0_attn_wo"], np.float32)),
        "L0_ffn_gate_w": np.ascontiguousarray(
            np.asarray(inputs["L0_ffn_gate_w"], np.int32)),
        "L0_ffn_up_w": np.ascontiguousarray(
            np.asarray(inputs["L0_ffn_up_w"], np.int32)),
        "L0_ffn_down_w": np.ascontiguousarray(
            np.asarray(inputs["L0_ffn_down_w"], np.int32)),
        "L0_ffn_gate_scale": np.ascontiguousarray(
            np.asarray(inputs["L0_ffn_gate_scale"], np.float32)),
        "L0_ffn_up_scale": np.ascontiguousarray(
            np.asarray(inputs["L0_ffn_up_scale"], np.float32)),
        "L0_ffn_down_scale": np.ascontiguousarray(
            np.asarray(inputs["L0_ffn_down_scale"], np.float32)),
        "gq_flat": gq, "gk_flat": np.ascontiguousarray(gk.reshape(-1)),
        "gk_rep": gk_rep, "maskA": maskA, "maskB": maskB,
        "id_f32": id_f, "id_bf16": id_b,
    }
    in_maps = []
    plans = []
    for c in range(NCORE):
        slots, prog = plan_core(c)
        plans.append((slots, prog))
        xo = np.zeros((2 * NSLOT, KB, D), np.float32)
        co = np.zeros((128, NSLOT * QB), np.float32)
        so = np.zeros((128, NSLOT * QB), np.float32)
        for j, b in enumerate(slots):
            xo[2 * j] = x[b * QB:b * QB + KB]
            xo[2 * j + 1] = x[b * QB + KB:(b + 1) * QB]
            co[:, j * QB:(j + 1) * QB] = cos_t[:, b * QB:(b + 1) * QB]
            so[:, j * QB:(j + 1) * QB] = sin_t[:, b * QB:(b + 1) * QB]
        pad = np.zeros((NSLOT, NKB + 2), np.float32)
        for j, b in enumerate(slots):
            for i in range(KS[j]):
                if i >= 2 * b:
                    pad[j, i] = 1.0
        padrep = np.ascontiguousarray(
            np.broadcast_to(pad.reshape(1, -1),
                            (128, NSLOT * (NKB + 2))))
        m = dict(shared)
        m.update({"x_own": xo, "cos_own": co, "sin_own": so,
                  "pad_rep": padrep})
        in_maps.append(m)
    return in_maps, plans


def kernel(**inputs):
    nc = _get_nc()
    in_maps, plans = make_inmaps(inputs)
    res = run_bass_kernel_spmd(nc, in_maps, core_ids=list(range(NCORE)))
    y = np.empty((T, D), np.float32)
    for c in range(NCORE):
        shard = np.asarray(res.results[c]["out_shard"])
        slots, _ = plans[c]
        for j, b in enumerate(slots):
            y[b * QB:b * QB + KB] = shard[2 * j]
            y[b * QB + KB:(b + 1) * QB] = shard[2 * j + 1]
    return y.reshape(1, T, D)


# version 16
# speedup vs baseline: 1.0056x; 1.0056x over previous
import math
import numpy as np
import ml_dtypes
from contextlib import ExitStack

import concourse.bass as bass
import concourse.mybir as mybir
from concourse.bass_utils import run_bass_kernel_spmd

F32 = mybir.dt.float32
BF16 = mybir.dt.bfloat16
I32 = mybir.dt.int32
AF = mybir.ActivationFunctionType
ALU = mybir.AluOpType
AXX = mybir.AxisListType.X
BFNP = ml_dtypes.bfloat16

D = 2048
T = 8192
DH = 128
G = 4
NH = 16
DI = 8192
NCORE = 8
KB = 128
NKB = T // KB
QB = 256
NSLOT = 4
KS = [64, 48, 32, 16]
EPS_LN = 1e-5
EPS_RMS = 1e-6
ROT = 10000.0
NEG = -1e30
HC = D // 128
NIC = DI // 128

DBG = 0


def plan_core(c):
    slots = [24 + c, 16 + c, 8 + c, c]
    prog = [None] * NKB
    for i in range(2 * c):
        prog[i] = i
    prog[14] = 2 * c
    prog[15] = 2 * c + 1
    for k in range(1, 4):
        base = 2 * c + 2 + 16 * (k - 1)
        for i in range(16):
            prog[16 * k + i] = base + i
    for j, b in enumerate(slots):
        P = 2 * b + 2
        seen = [p for p in prog[:KS[j]] if p is not None]
        assert sorted(seen) == list(range(P))
        assert prog[KS[j] - 2] == P - 2 and prog[KS[j] - 1] == P - 1
    return slots, prog


def host_tables():
    d = np.arange(64, dtype=np.float64)
    ang = ROT ** (-d / 64.0)
    tt = np.arange(T, dtype=np.float64)
    rad = np.outer(ang, tt)
    cos_t = np.concatenate([np.cos(rad)] * 2, 0).astype(np.float32)
    sin_t = np.concatenate([-np.sin(rad), np.sin(rad)], 0).astype(np.float32)
    p = np.arange(128)[:, None]
    f = np.arange(QB)[None, :]
    maskA = np.where(p > f, NEG, 0.0).astype(np.float32)
    maskB = np.where(p + 128 > f, NEG, 0.0).astype(np.float32)
    return cos_t, sin_t, maskA, maskB


class Sem:
    __slots__ = ("h", "n", "name")

    def __init__(self, nc, name):
        self.h = nc.alloc_semaphore(name=name)
        self.n = 0
        self.name = name


class Sched:
    ENGS = ("sync", "scalar", "vector", "tensor", "gpsimd")
    AUTO_BRIDGE = ("scalar", "vector")

    def __init__(self, nc):
        self.nc = nc
        self.ops = []
        self.auto = {}
        self.last_inc = {}

    def op(self, eng, fn, waits=(), incs=()):
        assert eng in self.ENGS, eng
        waits = [w for w in waits if w is not None]
        incs = list(incs)
        assert len(incs) <= 1, "one sync update per instruction"
        if eng in self.AUTO_BRIDGE:
            prev = self.last_inc.get(eng)
            if prev is not None and prev not in waits:
                waits.append(prev)
            if not incs:
                if eng not in self.auto:
                    self.auto[eng] = Sem(self.nc, f"auto_{eng}")
                incs = [self.inc(self.auto[eng])]
            self.last_inc[eng] = (incs[0][0], incs[0][0].n)
        self.ops.append((eng, fn, tuple(waits), tuple(incs)))

    def inc(self, sem, k=1):
        sem.n += k
        return (sem, k)

    def emit(self, name):
        nc = self.nc
        with nc.Block(name) as block:
            for eng_name in self.ENGS:
                ops = [o for o in self.ops if o[0] == eng_name]
                if not ops:
                    continue

                def body(eng, ops=ops):
                    for (_, fn, waits, incs) in ops:
                        for (s, tgt) in waits:
                            if tgt > 0:
                                eng.wait_ge(s.h, tgt)
                        inst = fn(eng)
                        for (s, k) in incs:
                            inst.then_inc(s.h, k)
                getattr(block, eng_name)(body)
        self.ops = []


def rsqrt_newton(sc, gate, hop, v_ap, ta, tb, out_inv, inv_n, eps,
                 done_sem):
    I = sc.inc
    sc.op("vector", lambda e: e.tensor_scalar(ta, v_ap, inv_n, eps,
                                              ALU.mult, ALU.add),
          waits=gate, incs=[I(hop)])

    def lnexp(e):
        e.activation(tb, ta, AF.Ln)
        e.drain()
        return e.activation(out_inv, tb, AF.Exp, scale=-0.5)
    sc.op("scalar", lnexp, waits=[(hop, hop.n)], incs=[I(done_sem)])


def ln_pre(sc, nc, xap, bns_ap, mv_ap, st_ap, eps, bn_waits, hop):
    I = sc.inc

    def bnst(e):
        r = None
        for q in range(4):
            r = e.bn_stats(bns_ap[:, q * 6:(q + 1) * 6],
                           xap[:, q * 512:(q + 1) * 512])
        return r
    sc.op("vector", bnst, waits=bn_waits, incs=[])
    sc.op("vector", lambda e: e.bn_aggr(mv_ap, bns_ap), incs=[I(hop)])


def ln_post(sc, nc, xap, mv_ap, st_ap, out_ap, eps, apply_waits, done_incs,
            hop):
    I = sc.inc
    inv = st_ap[:, 0:1]
    mb = st_ap[:, 1:2]
    ta = st_ap[:, 2:3]
    tb = st_ap[:, 3:4]
    rsqrt_newton(sc, [(hop, hop.n)], hop, mv_ap[:, 1:2], ta, tb, inv, 1.0,
                 eps, hop)

    def mbias(e):
        e.tensor_tensor(mb, mv_ap[:, 0:1], inv, ALU.mult)
        e.drain()
        return e.tensor_scalar(mb, mb, -1.0, None, ALU.mult)
    sc.op("vector", mbias, waits=[(hop, hop.n)], incs=[I(hop)])
    sc.op("scalar", lambda e: e.activation(out_ap, xap, AF.Identity,
                                           bias=mb, scale=inv),
          waits=[(hop, hop.n)] + list(apply_waits), incs=list(done_incs))


def ln_tile(sc, nc, xap, bns_ap, mv_ap, st_ap, out_ap, eps, bn_waits,
            apply_waits, done_incs, hop):
    ln_pre(sc, nc, xap, bns_ap, mv_ap, st_ap, eps, bn_waits, hop)
    ln_post(sc, nc, xap, mv_ap, st_ap, out_ap, eps, apply_waits, done_incs,
            hop)


def build_nc(dbg=DBG, phases=("A1", "A2", "B", "C", "D")):
    nc = bass.Bass("TRN2", target_bir_lowering=False)
    es = ExitStack()

    def sb(name, shape, dt):
        return es.enter_context(nc.sbuf_tensor(name, shape, dt))

    def dram_in(name, shape, dt):
        return nc.dram_tensor(name, shape, dt, kind="ExternalInput").ap()

    def dram_scr(name, shape, dt):
        kind = "ExternalOutput" if dbg else "Internal"
        return nc.dram_tensor(name, shape, dt, kind=kind).ap()

    ctx = {}
    ctx["wq"] = dram_in("L0_attn_wq", [D, D], F32)
    ctx["wk"] = dram_in("L0_attn_wk", [D, 512], F32)
    ctx["wv"] = dram_in("L0_attn_wv", [D, 512], F32)
    ctx["wo"] = dram_in("L0_attn_wo", [D, D], F32)
    ctx["gw"] = dram_in("L0_ffn_gate_w", [DI, D], I32)
    ctx["uw"] = dram_in("L0_ffn_up_w", [DI, D], I32)
    ctx["dw"] = dram_in("L0_ffn_down_w", [D, DI], I32)
    ctx["gs"] = dram_in("L0_ffn_gate_scale", [DI], F32)
    ctx["us"] = dram_in("L0_ffn_up_scale", [DI], F32)
    ctx["ds"] = dram_in("L0_ffn_down_scale", [D], F32)
    ctx["gq_f"] = dram_in("gq_flat", [NH * DH], F32)
    ctx["gk_f"] = dram_in("gk_flat", [G * DH], F32)
    ctx["gk_r"] = dram_in("gk_rep", [NH * DH], F32)
    ctx["xown"] = dram_in("x_own", [2 * NSLOT, KB, D], F32)
    ctx["coso"] = dram_in("cos_own", [128, NSLOT * QB], F32)
    ctx["sino"] = dram_in("sin_own", [128, NSLOT * QB], F32)
    ctx["padrep"] = dram_in("pad_rep", [128, NSLOT * (NKB + 2)], F32)
    ctx["maskA_i"] = dram_in("maskA", [128, QB], F32)
    ctx["maskB_i"] = dram_in("maskB", [128, QB], F32)
    ctx["idf_i"] = dram_in("id_f32", [128, 128], F32)
    ctx["idb_i"] = dram_in("id_bf16", [128, 128], BF16)

    ctx["out"] = nc.dram_tensor("out_shard", [2 * NSLOT, KB, D], F32,
                                kind="ExternalOutput").ap()
    ctx["kv_own"] = nc.dram_tensor("kv_own", [G, 8, 128, 512], BF16,
                                   kind="Internal").ap()
    ctx["kv_gath"] = nc.dram_tensor("kv_gath", [G, 8, 4, 2, 128, 512],
                                    BF16, kind="Internal").ap()
    ctx["q_scr"] = dram_scr("q_scr", [NSLOT, NH, 128, 512], BF16)
    ctx["o_scr"] = dram_scr("o_scr", [HC, NSLOT, 128, 512], BF16)
    ctx["x2_scr"] = dram_scr("x2_scr", [2 * NSLOT, 128, D], F32)

    ctx["c_idf"] = sb("c_idf", [128, 128], F32)
    ctx["c_idb"] = sb("c_idb", [128, 128], BF16)
    ctx["c_maskA"] = sb("c_maskA", [128, QB], F32)
    ctx["c_maskB"] = sb("c_maskB", [128, QB], F32)
    ctx["c_bias"] = sb("c_bias", [128, NSLOT * (NKB + 2)], F32)
    ctx["c_ones"] = sb("c_ones", [128, 1], BF16)
    ctx["c_onesf"] = sb("c_onesf", [128, 1], F32)
    ctx["c_onesrf"] = sb("c_onesrf", [1, 128], F32)
    ctx["c_n128"] = sb("c_n128", [1, 128], BF16)
    ctx["c_gq"] = sb("c_gq", [1, NH * DH], F32)
    ctx["c_gk"] = sb("c_gk", [1, G * DH], F32)
    ctx["c_sg"] = sb("c_sg", [128, NIC], F32)
    ctx["c_su"] = sb("c_su", [128, NIC], F32)
    ctx["c_sd"] = sb("c_sd", [128, HC], F32)
    ctx["c_crep"] = sb("c_crep", [128, 1], F32)
    ctx["c_tmp"] = sb("c_tmp", [128, NSLOT * (NKB + 2)], F32)

    sc = Sched(nc)
    build_setup(nc, sc, ctx)
    if "A1" in phases:
        build_a1(nc, sc, ctx)
    if "A2" in phases:
        build_a2(nc, sc, ctx)
    if "B" in phases:
        build_b(nc, sc, ctx)
    if "C" in phases:
        build_c(nc, sc, ctx)
    if "D" in phases:
        build_d(nc, sc, ctx)
    es.close()
    return nc


def build_setup(nc, sc, ctx):
    I = sc.inc
    s_ld = Sem(nc, "su_ld")
    s_scl = Sem(nc, "su_scl")
    s_c = Sem(nc, "su_c")
    with nc.psum_tensor("p_set", [128, 512], F32) as p_set, \
            nc.sbuf_tensor("su_gkr", [1, NH * DH], F32) as gkr:
        tmp = ctx["c_tmp"]
        loads = [
            (ctx["c_idf"][:], ctx["idf_i"][:, :]),
            (ctx["c_idb"][:], ctx["idb_i"][:, :]),
            (ctx["c_maskA"][:], ctx["maskA_i"][:, :]),
            (ctx["c_maskB"][:], ctx["maskB_i"][:, :]),
            (tmp[:], ctx["padrep"][:, :]),
            (ctx["c_gq"][:], ctx["gq_f"].rearrange("(o n) -> o n", o=1)),
            (ctx["c_gk"][:], ctx["gk_f"].rearrange("(o n) -> o n", o=1)),
            (gkr[:, :], ctx["gk_r"].rearrange("(o n) -> o n", o=1)),
        ]
        for dst, src in loads:
            sc.op("sync", lambda e, d=dst, s=src: e.dma_start(out=d, in_=s),
                  incs=[I(s_ld, 16)])
        for src, dst, n in ((ctx["gs"], ctx["c_sg"], DI),
                            (ctx["us"], ctx["c_su"], DI),
                            (ctx["ds"], ctx["c_sd"], D)):
            sc.op("scalar", lambda e, s=src, d=dst, n=n: e.dma_start(
                out=d[:, 0:n // 128],
                in_=s.rearrange("(c p) -> p c", p=128),
                allow_slow_non_contiguous=True), incs=[I(s_scl, 16)])
        ctx["scl_tok"] = (s_scl, s_scl.n)
        sc.op("gpsimd", lambda e: e.memset(ctx["c_ones"][:], 1.0),
              incs=[I(s_c)])
        sc.op("gpsimd", lambda e: e.memset(ctx["c_onesf"][:], 1.0),
              incs=[I(s_c)])
        sc.op("gpsimd", lambda e: e.memset(ctx["c_onesrf"][:], 1.0),
              incs=[I(s_c)])
        sc.op("gpsimd", lambda e: e.memset(ctx["c_n128"][:], -128.0),
              incs=[I(s_c)])
        nld = s_ld.n

        def cstar(e):
            e.tensor_tensor(gkr[:, :], ctx["c_gq"][:, :], gkr[:, :],
                            ALU.mult)
            e.drain()
            e.tensor_reduce(ctx["c_crep"][0:1, 0:1], gkr[:, :],
                            AXX, ALU.max, apply_absolute_value=True)
            e.drain()
            return e.tensor_scalar(ctx["c_crep"][0:1, 0:1],
                                   ctx["c_crep"][0:1, 0:1],
                                   math.sqrt(128.0), None, ALU.mult)
        sc.op("vector", cstar, waits=[(s_ld, nld)], incs=[I(s_c)])
        sc.op("tensor", lambda e: e.matmul(
            p_set[:, 0:1], ctx["c_onesrf"][:, :], ctx["c_crep"][0:1, 0:1],
            start=True, stop=True), waits=[(s_c, s_c.n)], incs=[I(s_c)])
        sc.op("vector", lambda e: e.tensor_copy(ctx["c_crep"][:, :],
                                                p_set[:, 0:1]),
              waits=[(s_c, s_c.n)], incs=[I(s_c)])
        sc.op("vector", lambda e: e.tensor_scalar(
            ctx["c_bias"][:, :], tmp[:, :], NEG, ctx["c_crep"][:, 0:1],
            ALU.mult, ALU.subtract), waits=[(s_c, s_c.n)], incs=[I(s_c)])
        sc.emit("setup")


def build_a1(nc, sc, ctx):
    I = sc.inc
    st = ExitStack()

    def sbl(n, s, d):
        return st.enter_context(nc.sbuf_tensor(n, s, d))

    def psl(n, s, d):
        return st.enter_context(nc.psum_tensor(n, s, d))

    xt = sbl("a1_xt", [128, 2 * D], F32)
    trig = sbl("a1_trig", [128, 4 * 256], F32)
    bns = sbl("a1_bns", [128, 2 * 24], F32)
    mv = sbl("a1_mv", [128, 2 * 2], F32)
    stt = sbl("a1_stt", [128, 2 * 4], F32)
    lnc = sbl("a1_lnc", [128, 4 * D], F32)
    xfh = sbl("a1_xfh", [128, 2 * 4096], BF16)
    xfl = sbl("a1_xfl", [128, 2 * 4096], BF16)
    wkvh = sbl("a1_wkvh", [128, HC * 1024], BF16)
    wkvl = sbl("a1_wkvl", [128, HC * 1024], BF16)
    wldf = sbl("a1_wldf", [128, 2 * 1024], F32)
    kp = sbl("a1_kp", [128, 8 * 256], F32)
    ksqb = sbl("a1_ksqb", [128, 256], F32)
    vst = sbl("a1_vst", [128, 4 * 256], BF16)
    rr = sbl("a1_rr", [1, 3 * 256], F32)
    kvo = sbl("a1_kvo", [128, 2 * G * 1024], BF16)

    pxT = [psl("a1_pxT0", [128, 512], F32), psl("a1_pxT1", [128, 512], F32)]
    pkv = [psl(f"a1_pkv{i}", [128, 512], F32) for i in range(4)]
    prow = psl("a1_prow", [128, 512], F32)
    pvT = psl("a1_pvT", [128, 1024], BF16)

    s_ldw = [Sem(nc, "a1_ldw0"), Sem(nc, "a1_ldw1")]
    s_wsp = Sem(nc, "a1_wsp")
    s_ldx = [Sem(nc, f"a1_ldx{i}") for i in range(4)]
    s_trl = [Sem(nc, "a1_trl0"), Sem(nc, "a1_trl1")]
    hop = Sem(nc, "a1_hop")
    s_ln = Sem(nc, "a1_ln")
    s_xT = Sem(nc, "a1_xT")
    s_xfh = Sem(nc, "a1_xfh")
    s_xfl = Sem(nc, "a1_xfl")
    s_kv = Sem(nc, "a1_kv")
    s_pe = Sem(nc, "a1_pe")
    s_kc = Sem(nc, "a1_kc")
    s_ke = Sem(nc, "a1_ke")
    s_sw = Sem(nc, "a1_sw")
    s_stg = Sem(nc, "a1_stg")
    s_sto = [Sem(nc, "a1_sto0"), Sem(nc, "a1_sto1")]
    hop2 = Sem(nc, "a1_hop2")

    wldf_free = {}
    for hc in range(HC):
        b = hc % 2
        w = [wldf_free.get(hc - 2)]
        sc.op("sync", lambda e, hc=hc, b=b: e.dma_start(
            out=wldf[:, b * 1024:b * 1024 + 512],
            in_=ctx["wk"][hc * 128:(hc + 1) * 128, :]), waits=w,
            incs=[I(s_ldw[b], 16)])
        sc.op("sync", lambda e, hc=hc, b=b: e.dma_start(
            out=wldf[:, b * 1024 + 512:b * 1024 + 1024],
            in_=ctx["wv"][hc * 128:(hc + 1) * 128, :]),
            incs=[I(s_ldw[b], 16)])
        sc.op("vector", lambda e, hc=hc, b=b: e.tensor_copy(
            wkvh[:, hc * 1024:(hc + 1) * 1024],
            wldf[:, b * 1024:(b + 1) * 1024]),
            waits=[(s_ldw[b], s_ldw[b].n)], incs=[I(s_wsp)])
        sc.op("vector", lambda e, hc=hc, b=b: e.tensor_tensor(
            wkvl[:, hc * 1024:(hc + 1) * 1024],
            wldf[:, b * 1024:(b + 1) * 1024],
            wkvh[:, hc * 1024:(hc + 1) * 1024], ALU.subtract),
            incs=[I(s_wsp)])
        wldf_free[hc] = (s_wsp, s_wsp.n)
    wsp_done = s_wsp.n

    NSB = NSLOT
    ln_tok = []
    xT_tok = []
    rope_tok = []
    bankfree = {}
    stg_tok = []
    sto_tok = []
    ctxA = dict(xt=xt, trig=trig, kp=kp, ksqb=ksqb, vst=vst, rr=rr, kvo=kvo,
                pkv=pkv, prow=prow, pvT=pvT, pxT=pxT, s_kv=s_kv, s_pe=s_pe,
                s_kc=s_kc, s_ke=s_ke, s_sw=s_sw, s_stg=s_stg,
                bankfree=bankfree, hop2=hop2, s_trl=s_trl)
    trl_toks = {}

    def emit_loads(s):
        pp = s % 2
        for ti in range(2):
            w = [(s_ln, ln_tok[s - 1][1]) if s >= 1 else None]
            sc.op("sync", lambda e, s=s, ti=ti, pp=pp: e.dma_start(
                out=xt[:, ti * D:(ti + 1) * D],
                in_=ctx["xown"][2 * s + ti, :, :]), waits=w,
                incs=[I(s_ldx[pp * 2 + ti], 16)])
        w = [rope_tok[s - 2] if s >= 2 else None]
        sc.op("sync", lambda e, s=s, pp=pp: e.dma_start(
            out=trig[:, pp * 512:pp * 512 + 256],
            in_=ctx["coso"][:, s * 256:(s + 1) * 256]), waits=w,
            incs=[I(s_trl[pp], 16)])
        sc.op("sync", lambda e, s=s, pp=pp: e.dma_start(
            out=trig[:, pp * 512 + 256:pp * 512 + 512],
            in_=ctx["sino"][:, s * 256:(s + 1) * 256]),
            incs=[I(s_trl[pp], 16)])
        trl_toks[s] = (s_trl[pp], s_trl[pp].n)

    def emit_ln(s, stage):
        pp = s % 2
        for ti in range(2):
            xap = xt[:, ti * D:(ti + 1) * D]
            if stage == 0:
                bn_w = [(s_ldx[pp * 2 + ti], s_ldx[pp * 2 + ti].n)]
                ln_pre(sc, nc, xap, bns[:, ti * 24:(ti + 1) * 24],
                       mv[:, ti * 2:(ti + 1) * 2],
                       stt[:, ti * 4:(ti + 1) * 4], EPS_LN, bn_w, hop)
            else:
                ap_w = [(s_xT, xT_tok[s - 2][1]) if s >= 2 else None]
                ap_w = [w for w in ap_w if w]
                ln_post(sc, nc, xap, mv[:, ti * 2:(ti + 1) * 2],
                        stt[:, ti * 4:(ti + 1) * 4],
                        lnc[:, (pp * 2 + ti) * D:(pp * 2 + ti + 1) * D],
                        EPS_LN, ap_w, [I(s_ln)], hop)
        if stage == 1:
            ln_tok.append((s_ln, s_ln.n))

    emit_loads(0)
    emit_ln(0, 0)
    emit_ln(0, 1)
    for s in range(NSB):
        pp = s % 2
        ctxA["trl_tok"] = trl_toks[s]
        if s + 1 < NSB:
            emit_loads(s + 1)

        for r in range(8):
            rg = s * 8 + r

            def tr(e, r=r, pp=pp):
                rr_ = None
                for k in range(4):
                    hc = 2 * r + (k // 2)
                    ti = k % 2
                    rr_ = e.transpose(pxT[r % 2][:, k * 128:(k + 1) * 128],
                                      lnc[:, (pp * 2 + ti) * D + hc * 128:
                                          (pp * 2 + ti) * D + (hc + 1) * 128],
                                      ctx["c_idf"][:, :])
                return rr_
            w = [(s_ln, s_ln.n)]
            if rg >= 2:
                w.append((s_xfh, rg - 1))
                w.append((s_xfl, rg - 1))
            if r < 2 and s >= 1:
                w.append(rope_tok[s - 1])
            sc.op("tensor", tr, waits=w, incs=[I(s_xT)])
            dst = pp * 4096 + r * 512
            sc.op("scalar", lambda e, r=r, dst=dst: e.activation(
                xfh[:, dst:dst + 512], pxT[r % 2][:, :], AF.Copy, bias=0.0,
                scale=1.0), waits=[(s_xT, s_xT.n)], incs=[I(s_xfh)])
            sc.op("vector", lambda e, r=r, dst=dst: e.tensor_tensor(
                xfl[:, dst:dst + 512], pxT[r % 2][:, :],
                xfh[:, dst:dst + 512], ALU.subtract),
                waits=[(s_xfh, s_xfh.n)], incs=[I(s_xfl)])
        xT_tok.append((s_xT, s_xT.n))
        xf_done = [(s_xfh, s_xfh.n), (s_xfl, s_xfl.n)]
        if s + 1 < NSB:
            emit_ln(s + 1, 0)
            emit_ln(s + 1, 1)

        posts = []

        def stage1(i):
            poc, pgidx, ptok, pgate = posts[i]
            ctxA["kv_tok"] = ptok
            if poc < G:
                _a1_kpost1(nc, sc, ctx, ctxA, s, pp, poc, pgidx)
            else:
                _a1_vpost1(nc, sc, ctx, ctxA, s, pp, poc, pgidx)

        def stage2(i):
            poc, pgidx, ptok, pgate = posts[i]
            if poc < G:
                _a1_kpost2(nc, sc, ctx, ctxA, s, pp, poc, pgidx, pgate)
            else:
                _a1_vpost2(nc, sc, ctx, ctxA, s, pp, poc, pgidx, pgate)

        for oc in range(8):
            gidx = s * 8 + oc
            bank = pkv[oc % 4]

            def kvmm(e, oc=oc, bank=bank, pp=pp):
                r = None
                for hc in range(HC):
                    ws = slice(hc * 1024 + oc * 128,
                               hc * 1024 + (oc + 1) * 128)
                    xs = slice(pp * 4096 + hc * 256,
                               pp * 4096 + (hc + 1) * 256)
                    e.matmul(bank[:, 0:256], wkvh[:, ws], xfh[:, xs],
                             start=(hc == 0), stop=False)
                    e.matmul(bank[:, 0:256], wkvl[:, ws], xfh[:, xs],
                             start=False, stop=False)
                    r = e.matmul(bank[:, 0:256], wkvh[:, ws], xfl[:, xs],
                                 start=False, stop=(hc == HC - 1))
                return r
            w = list(xf_done) + [(s_wsp, wsp_done)]
            w += bankfree.get(gidx - 4, [])
            sc.op("tensor", kvmm, waits=w, incs=[I(s_kv)])
            stg_gate = sto_tok[s - 2] if s >= 2 else None
            posts.append((oc, gidx, s_kv.n, stg_gate))
            if oc >= 1:
                stage1(oc - 1)
            if oc >= 2:
                stage2(oc - 2)
        stage1(7)
        stage2(6)
        stage2(7)
        rope_tok.append((s_kc, s_kc.n))
        stg_tok.append((s_stg, s_stg.n))

        for g in range(G):
            base = pp * (G * 1024) + g * 1024
            w = [(s_stg, s_stg.n)]
            sc.op("gpsimd", lambda e, s=s, g=g, base=base: e.dma_start(
                out=ctx["kv_own"][g, 6 - 2 * s:8 - 2 * s, :, :].rearrange(
                    "b p f -> p b f"),
                in_=kvo[:, base:base + 1024].rearrange(
                    "p (b f) -> p b f", b=2)), waits=w,
                incs=[I(s_sto[pp], 16)])
        sto_tok.append((s_sto[pp], s_sto[pp].n))

    s_cc = Sem(nc, "a1_cc")
    ctx["s_cc"] = s_cc
    for g in range(G):
        w = ([(s_sto[0], s_sto[0].n), (s_sto[1], s_sto[1].n)]
             if g == 0 else [])
        sc.op("gpsimd", lambda e, g=g: e.collective_compute(
            "AllGather", ALU.bypass,
            replica_groups=[list(range(NCORE))],
            ins=[ctx["kv_own"][g, :, :, :]],
            outs=[ctx["kv_gath"][g].rearrange("c m b p f -> (c m b) p f")]),
            waits=w, incs=[I(s_cc)])

    sc.op("sync", lambda e: e.nop(),
          waits=[(s_sto[0], s_sto[0].n), (s_sto[1], s_sto[1].n)])
    sc.emit("A1")
    st.close()


def _a1_kpost1(nc, sc, ctx, A, s, pp, g, gidx):
    I = sc.inc
    bank = A["pkv"][g % 4]
    kp, ksqb, prow = A["kp"], A["ksqb"], A["prow"]
    s_kv, s_pe, s_kc, s_ke = A["s_kv"], A["s_pe"], A["s_kc"], A["s_ke"]
    kf = kp[:, (g % 2) * 256:(g % 2) * 256 + 256]
    sc.op("scalar", lambda e, kf=kf: e.activation(
        kf, bank[:, 0:256], AF.Copy, bias=0.0, scale=1.0),
        waits=[(s_kv, A["kv_tok"]), (s_kc, s_kc.n)], incs=[I(s_ke)])
    A["bankfree"][gidx] = [(s_ke, s_ke.n)]
    sc.op("vector", lambda e, kf=kf: e.tensor_tensor(ksqb[:, :], kf, kf,
                                                     ALU.mult),
          waits=[(s_ke, s_ke.n), (s_pe, s_pe.n)], incs=[I(s_kc)])
    pxT = A["pxT"]
    sc.op("tensor", lambda e, g=g: e.matmul(
        pxT[g % 2][0:1, 256:512], ctx["c_onesf"][:, :], ksqb[:, :],
        start=True, stop=True),
        waits=[(s_kc, s_kc.n)], incs=[I(s_pe)])
    A["ssq_tok" + str(g % 2)] = (s_pe, s_pe.n)


def _a1_kpost2(nc, sc, ctx, A, s, pp, g, gidx, stg_gate):
    I = sc.inc
    kp, rr, trig, kvo, prow = (A["kp"], A["rr"], A["trig"], A["kvo"],
                               A["prow"])
    s_pe, s_kc, s_ke, s_sw, s_stg = (A["s_pe"], A["s_kc"], A["s_ke"],
                                     A["s_sw"], A["s_stg"])
    hop2 = A["hop2"]
    pxT = A["pxT"]
    kf = kp[:, (g % 2) * 256:(g % 2) * 256 + 256]
    bc = kp[:, 512:768]
    kh = kp[:, 768:1024]
    ksw = kp[:, 1024:1280]
    t1 = kp[:, 1280:1536]
    krot = kp[:, 1536:1792]
    rsqrt_newton(sc, [A["ssq_tok" + str(g % 2)]], hop2,
                 pxT[g % 2][0:1, 256:512],
                 rr[0:1, 0:256], rr[0:1, 256:512], rr[0:1, 512:768],
                 1.0 / DH, EPS_RMS, s_kc)
    sc.op("tensor", lambda e, g=g: e.matmul(
        pxT[g % 2][:, 0:256], ctx["c_gk"][0:1, g * 128:(g + 1) * 128],
        rr[0:1, 512:768], start=True, stop=True),
        waits=[(s_kc, s_kc.n)], incs=[I(s_pe)])
    sc.op("vector", lambda e, g=g: e.tensor_copy(bc, pxT[g % 2][:, 0:256]),
          waits=[(s_pe, s_pe.n)], incs=[I(s_kc)])
    sc.op("vector", lambda e, kf=kf: e.tensor_tensor(kh, kf, bc, ALU.mult),
          waits=[(s_sw, s_sw.n)], incs=[I(s_kc)])
    sc.op("gpsimd", lambda e: e.dma_start(out=ksw[0:64, :],
                                          in_=kh[64:128, :]),
          waits=[(s_kc, s_kc.n)], incs=[I(s_sw, 16)])
    sc.op("gpsimd", lambda e: e.dma_start(out=ksw[64:128, :],
                                          in_=kh[0:64, :]),
          incs=[I(s_sw, 16)])

    def rope(e, pp=pp):
        e.tensor_tensor(t1, kh, trig[:, pp * 512:pp * 512 + 256], ALU.mult)
        e.tensor_tensor(krot, ksw, trig[:, pp * 512 + 256:pp * 512 + 512],
                        ALU.mult)
        e.drain()
        return e.tensor_tensor(krot, krot, t1, ALU.add)
    sc.op("vector", rope, waits=[(s_sw, s_sw.n), A["trl_tok"]],
          incs=[I(s_kc)])

    def split(e, pp=pp, g=g):
        for b in range(2):
            base = pp * (G * 1024) + g * 1024 + b * 512
            e.tensor_copy(kvo[:, base:base + 128],
                          krot[:, b * 128:(b + 1) * 128])
        e.drain()
        r = None
        for b in range(2):
            base = pp * (G * 1024) + g * 1024 + b * 512
            r = e.tensor_tensor(kvo[:, base + 128:base + 256],
                                krot[:, b * 128:(b + 1) * 128],
                                kvo[:, base:base + 128], ALU.subtract)
        return r
    sc.op("vector", split, waits=[stg_gate] if stg_gate else [],
          incs=[I(s_stg)])


def _a1_vpost1(nc, sc, ctx, A, s, pp, oc, gidx):
    I = sc.inc
    bank = A["pkv"][oc % 4]
    vst = A["vst"]
    s_kv, s_pe, s_kc, s_ke = A["s_kv"], A["s_pe"], A["s_kc"], A["s_ke"]
    vo = (oc % 2) * 512
    vh = vst[:, vo:vo + 256]
    vl = vst[:, vo + 256:vo + 512]
    sc.op("scalar", lambda e, vh=vh: e.activation(
        vh, bank[:, 0:256], AF.Copy, bias=0.0, scale=1.0),
        waits=[(s_kv, A["kv_tok"]), (s_pe, s_pe.n), (s_kc, s_kc.n)],
        incs=[I(s_ke)])
    sc.op("vector", lambda e, vh=vh, vl=vl: e.tensor_tensor(
        vl, bank[:, 0:256], vh, ALU.subtract),
        waits=[(s_ke, s_ke.n)], incs=[I(s_kc)])
    A["bankfree"][gidx] = [(s_ke, s_ke.n), (s_kc, s_kc.n)]
    A["v_tok" + str(oc % 2)] = (s_kc, s_kc.n)


def _a1_vpost2(nc, sc, ctx, A, s, pp, oc, gidx, stg_gate):
    I = sc.inc
    g = oc - G
    vst, pvT, kvo = A["vst"], A["pvT"], A["kvo"]
    s_pe, s_kc, s_stg = A["s_pe"], A["s_kc"], A["s_stg"]
    vo = (oc % 2) * 512
    vh = vst[:, vo:vo + 256]
    vl = vst[:, vo + 256:vo + 512]

    def trs(e, vh=vh, vl=vl):
        r = None
        for b in range(2):
            r = e.transpose(pvT[:, (2 * b) * 128:(2 * b + 1) * 128],
                            vh[:, b * 128:(b + 1) * 128], ctx["c_idb"][:, :])
            r = e.transpose(pvT[:, (2 * b + 1) * 128:(2 * b + 2) * 128],
                            vl[:, b * 128:(b + 1) * 128], ctx["c_idb"][:, :])
        return r
    sc.op("tensor", trs, waits=[A["v_tok" + str(oc % 2)],
                                (A["s_stg"], A["s_stg"].n)], incs=[I(s_pe)])

    def evac(e, pp=pp, g=g):
        r = None
        for b in range(2):
            base = pp * (G * 1024) + g * 1024 + b * 512 + 256
            e.tensor_copy(kvo[:, base:base + 128],
                          pvT[:, (2 * b) * 128:(2 * b + 1) * 128])
            r = e.tensor_copy(kvo[:, base + 128:base + 256],
                              pvT[:, (2 * b + 1) * 128:(2 * b + 2) * 128])
        return r
    w = [(s_pe, s_pe.n)]
    if stg_gate:
        w.append(stg_gate)
    sc.op("vector", evac, waits=w, incs=[I(s_stg)])



def build_a2(nc, sc, ctx):
    I = sc.inc
    st = ExitStack()

    def sbl(n, s, d):
        return st.enter_context(nc.sbuf_tensor(n, s, d))

    def psl(n, s, d):
        return st.enter_context(nc.psum_tensor(n, s, d))

    xt = sbl("a2_xt", [128, 2 * D], F32)
    trig = sbl("a2_trig", [128, 2 * 512], F32)
    bns = sbl("a2_bns", [128, 24], F32)
    mv = sbl("a2_mv", [128, 2], F32)
    stt = sbl("a2_stt", [128, 4], F32)
    lnc = sbl("a2_lnc", [128, 2 * D], F32)
    xqh = sbl("a2_xqh", [128, HC * 1024], BF16)
    xql = sbl("a2_xql", [128, HC * 1024], BF16)
    wqh = sbl("a2_wqh", [128, HC * 1024], BF16)
    wql = sbl("a2_wql", [128, HC * 1024], BF16)
    wldf = sbl("a2_wldf", [128, 2 * 1024], F32)
    kp = sbl("a2_kp", [128, 8 * 256], F32)
    ksqb = sbl("a2_ksqb", [128, 256], F32)
    rr = sbl("a2_rr", [1, 3 * 256], F32)
    qvo = sbl("a2_qvo", [128, 2 * 512], BF16)

    pxT = [psl("a2_pxT0", [128, 512], F32), psl("a2_pxT1", [128, 512], F32)]
    pq = [psl(f"a2_pq{i}", [128, 512], F32) for i in range(4)]
    qb = [psl("a2_qb0", [128, 512], F32), psl("a2_qb1", [128, 512], F32)]

    s_ldx = [Sem(nc, "a2_ldx0"), Sem(nc, "a2_ldx1")]
    s_trg = [Sem(nc, "a2_trg0"), Sem(nc, "a2_trg1")]
    hop = Sem(nc, "a2_hop")
    s_ln = Sem(nc, "a2_ln")
    s_xT = Sem(nc, "a2_xT")
    s_xfh = Sem(nc, "a2_xfh")
    s_xfl = Sem(nc, "a2_xfl")
    s_ldw = [Sem(nc, "a2_ldw0"), Sem(nc, "a2_ldw1")]
    s_wsp = Sem(nc, "a2_wsp")
    s_q = Sem(nc, "a2_q")
    s_pe = Sem(nc, "a2_pe")
    s_kc = Sem(nc, "a2_kc")
    s_ke = Sem(nc, "a2_ke")
    s_sw = Sem(nc, "a2_sw")
    s_stg = Sem(nc, "a2_stg")
    s_sto = [Sem(nc, "a2_sto0"), Sem(nc, "a2_sto1")]
    hop2 = Sem(nc, "a2_hop2")

    xT_tok = []
    ln_tok = []

    def emit_prep(tl):
        b = tl % 2
        w = [(s_ln, ln_tok[tl - 2][1]) if tl >= 2 else None]
        sc.op("sync", lambda e, tl=tl, b=b: e.dma_start(
            out=xt[:, b * D:(b + 1) * D], in_=ctx["xown"][tl, :, :]),
            waits=w, incs=[I(s_ldx[b], 16)])
        bn_w = [(s_ldx[b], s_ldx[b].n)]
        ap_w = [xT_tok[tl - 2] if tl >= 2 else None]
        ap_w = [w_ for w_ in ap_w if w_]
        ln_tile(sc, nc, xt[:, b * D:(b + 1) * D], bns[:, :], mv[:, :],
                stt[:, :], lnc[:, b * D:(b + 1) * D], EPS_LN, bn_w, ap_w,
                [I(s_ln)], hop)
        ln_tok.append((s_ln, s_ln.n))
        for r in range(8):
            rg = tl * 8 + r

            def tr(e, r=r, b=b):
                rr_ = None
                for k in range(2):
                    hc = 2 * r + k
                    rr_ = e.transpose(
                        pxT[(tl * 8 + r) % 2][:, k * 128:(k + 1) * 128],
                        lnc[:, b * D + hc * 128:b * D + (hc + 1) * 128],
                        ctx["c_idf"][:, :])
                return rr_
            w = [(s_ln, s_ln.n)]
            if rg >= 2:
                w.append((s_xfh, rg - 1))
                w.append((s_xfl, rg - 1))
            sc.op("tensor", tr, waits=w, incs=[I(s_xT)])

            def ehi(e, r=r, tl=tl):
                rr_ = None
                for k in range(2):
                    hc = 2 * r + k
                    dst = hc * 1024 + (tl // 2) * 256 + (tl % 2) * 128
                    rr_ = e.activation(xqh[:, dst:dst + 128],
                                       pxT[(tl * 8 + r) % 2][:, k * 128:
                                                             (k + 1) * 128],
                                       AF.Copy, bias=0.0, scale=1.0)
                return rr_
            sc.op("scalar", ehi, waits=[(s_xT, s_xT.n)], incs=[I(s_xfh)])

            def elo(e, r=r, tl=tl):
                rr_ = None
                for k in range(2):
                    hc = 2 * r + k
                    dst = hc * 1024 + (tl // 2) * 256 + (tl % 2) * 128
                    rr_ = e.tensor_tensor(
                        xql[:, dst:dst + 128],
                        pxT[(tl * 8 + r) % 2][:, k * 128:(k + 1) * 128],
                        xqh[:, dst:dst + 128], ALU.subtract)
                return rr_
            sc.op("vector", elo, waits=[(s_xfh, s_xfh.n)],
                  incs=[I(s_xfl)])
        xT_tok.append((s_xT, s_xT.n))

    xf_gate = {}
    emit_prep(0)
    emit_prep(1)
    xf_gate[0] = [(s_xfh, s_xfh.n), (s_xfl, s_xfl.n)]

    bankfree = {}
    sto_tok = {}
    stg_tok = {}
    qpost_i = 0
    for half in range(2):
        wldf_free = {}
        half_gate = (s_q, s_q.n) if half else None
        for hc in range(HC):
            b = hc % 2
            w = [wldf_free.get(hc - 2)]
            if half_gate:
                w.append(half_gate)
            sc.op("sync", lambda e, half=half, hc=hc, b=b: e.dma_start(
                out=wldf[:, b * 1024:(b + 1) * 1024],
                in_=ctx["wq"][hc * 128:(hc + 1) * 128,
                              half * 1024:(half + 1) * 1024]), waits=w,
                incs=[I(s_ldw[b], 16)])
            sc.op("vector", lambda e, hc=hc, b=b: e.tensor_copy(
                wqh[:, hc * 1024:(hc + 1) * 1024],
                wldf[:, b * 1024:(b + 1) * 1024]),
                waits=[(s_ldw[b], s_ldw[b].n)], incs=[I(s_wsp)])
            sc.op("vector", lambda e, hc=hc, b=b: e.tensor_tensor(
                wql[:, hc * 1024:(hc + 1) * 1024],
                wldf[:, b * 1024:(b + 1) * 1024],
                wqh[:, hc * 1024:(hc + 1) * 1024], ALU.subtract),
                incs=[I(s_wsp)])
            wldf_free[hc] = (s_wsp, s_wsp.n)
        wsp_half = s_wsp.n

        post_q = []
        A2d = dict(kp=kp, ksqb=ksqb, rr=rr, trig=trig, qvo=qvo, qb=qb,
                   s_q=s_q, s_pe=s_pe, s_kc=s_kc, s_ke=s_ke,
                   s_sw=s_sw, s_stg=s_stg, s_sto=s_sto, s_trg=s_trg,
                   bankfree=bankfree, sto_tok=sto_tok, hop2=hop2)

        def pop_post(stage):
            if stage == 1:
                (poc, pslot, pidx, pbank, ptok) = post_q[-2]
                A2d["bank"] = pbank
                A2d["q_tok"] = ptok
                _a2_qpost1(nc, sc, ctx, A2d, poc, pslot, pidx)
            else:
                (poc, pslot, pidx, pbank, ptok) = post_q.pop(0)
                _a2_qpost2(nc, sc, ctx, A2d, poc, pslot, pidx)

        for slot in range(NSLOT):
            if half == 0 and slot + 1 < NSLOT:
                emit_prep(2 * (slot + 1))
                emit_prep(2 * (slot + 1) + 1)
                xf_gate[slot + 1] = [(s_xfh, s_xfh.n), (s_xfl, s_xfl.n)]
            for oci in range(8):
                oc = half * 8 + oci
                idx = qpost_i
                qpost_i += 1
                bank = pq[idx % 4]

                def qmm(e, oci=oci, slot=slot, bank=bank):
                    r = None
                    for hc in range(HC):
                        ws = slice(hc * 1024 + oci * 128,
                                   hc * 1024 + (oci + 1) * 128)
                        xs = slice(hc * 1024 + slot * 256,
                                   hc * 1024 + (slot + 1) * 256)
                        e.matmul(bank[:, 0:256], wqh[:, ws], xqh[:, xs],
                                 start=(hc == 0), stop=False)
                        e.matmul(bank[:, 0:256], wql[:, ws], xqh[:, xs],
                                 start=False, stop=False)
                        r = e.matmul(bank[:, 0:256], wqh[:, ws], xql[:, xs],
                                     start=False, stop=(hc == HC - 1))
                    return r
                w = list(xf_gate[slot]) + [(s_wsp, wsp_half)]
                w += bankfree.get(idx - 4, [])
                sc.op("tensor", qmm, waits=w, incs=[I(s_q)])
                post_q.append((oc, slot, idx, bank, s_q.n))
                if len(post_q) >= 2:
                    pop_post(1)
                if len(post_q) >= 3:
                    pop_post(2)
        post_q.append(post_q[-1])
        pop_post(1)
        pop_post(2)
        pop_post(2)
        post_q.clear()

    sc.op("sync", lambda e: e.nop(),
          waits=[(s_sto[0], s_sto[0].n), (s_sto[1], s_sto[1].n)])
    sc.emit("A2")
    st.close()


def rg_mod2(x):
    return x % 2


def _a2_qpost1(nc, sc, ctx, A, oc, slot, idx):
    I = sc.inc
    kp, ksqb, qb, bank = A["kp"], A["ksqb"], A["qb"], A["bank"]
    s_q, s_pe, s_kc, s_ke = A["s_q"], A["s_pe"], A["s_kc"], A["s_ke"]
    kf = kp[:, (idx % 2) * 256:(idx % 2) * 256 + 256]
    sc.op("scalar", lambda e, kf=kf: e.activation(
        kf, bank[:, 0:256], AF.Copy, bias=0.0, scale=1.0),
        waits=[(s_q, A["q_tok"]), (s_kc, s_kc.n)], incs=[I(s_ke)])
    A["bankfree"][idx] = [(s_ke, s_ke.n)]
    sc.op("vector", lambda e, kf=kf: e.tensor_tensor(ksqb[:, :], kf, kf,
                                                     ALU.mult),
          waits=[(s_ke, s_ke.n), (s_pe, s_pe.n)], incs=[I(s_kc)])
    sc.op("tensor", lambda e, idx=idx: e.matmul(
        qb[idx % 2][0:1, 256:512], ctx["c_onesf"][:, :], ksqb[:, :],
        start=True, stop=True), waits=[(s_kc, s_kc.n)], incs=[I(s_pe)])
    A["ssq_tok" + str(idx % 2)] = (s_pe, s_pe.n)


def _a2_qpost2(nc, sc, ctx, A, oc, slot, idx):
    I = sc.inc
    kp, rr, trig, qvo, qb = (A["kp"], A["rr"], A["trig"], A["qvo"],
                             A["qb"])
    s_pe, s_kc, s_ke, s_sw, s_stg, s_sto, s_trg = (
        A["s_pe"], A["s_kc"], A["s_ke"], A["s_sw"], A["s_stg"], A["s_sto"],
        A["s_trg"])
    hop2 = A["hop2"]
    kf = kp[:, (idx % 2) * 256:(idx % 2) * 256 + 256]
    bc = kp[:, 512:768]
    kh = kp[:, 768:1024]
    ksw = kp[:, 1024:1280]
    t1 = kp[:, 1280:1536]
    krot = kp[:, 1536:1792]
    pp = idx % 2
    ro = pp * 256

    sc.op("sync", lambda e, slot=slot, pp=pp: e.dma_start(
        out=trig[:, pp * 512:pp * 512 + 256],
        in_=ctx["coso"][:, slot * 256:(slot + 1) * 256]),
        waits=[(s_sw, s_sw.n)], incs=[I(s_trg[pp], 16)])
    sc.op("sync", lambda e, slot=slot, pp=pp: e.dma_start(
        out=trig[:, pp * 512 + 256:pp * 512 + 512],
        in_=ctx["sino"][:, slot * 256:(slot + 1) * 256]),
        incs=[I(s_trg[pp], 16)])
    trg_tok = (s_trg[pp], s_trg[pp].n)

    rsqrt_newton(sc, [A["ssq_tok" + str(pp)]], hop2,
                 qb[pp][0:1, 256:512],
                 rr[0:1, 0:256], rr[0:1, 256:512], rr[0:1, 512:768],
                 1.0 / DH, EPS_RMS, s_kc)
    sc.op("tensor", lambda e, oc=oc, pp=pp: e.matmul(
        qb[pp][:, 0:256], ctx["c_gq"][0:1, oc * 128:(oc + 1) * 128],
        rr[0:1, 512:768], start=True, stop=True),
        waits=[(s_kc, s_kc.n)], incs=[I(s_pe)])
    sc.op("vector", lambda e, pp=pp: e.tensor_copy(bc, qb[pp][:, 0:256]),
          waits=[(s_pe, s_pe.n)], incs=[I(s_kc)])
    sc.op("vector", lambda e, kf=kf: e.tensor_tensor(kh, kf, bc, ALU.mult),
          waits=[(s_sw, s_sw.n)], incs=[I(s_kc)])
    sc.op("gpsimd", lambda e: e.dma_start(out=ksw[0:64, :],
                                          in_=kh[64:128, :]),
          waits=[(s_kc, s_kc.n)], incs=[I(s_sw, 16)])
    sc.op("gpsimd", lambda e: e.dma_start(out=ksw[64:128, :],
                                          in_=kh[0:64, :]),
          incs=[I(s_sw, 16)])

    def rope(e, pp=pp):
        e.tensor_tensor(t1, kh, trig[:, pp * 512:pp * 512 + 256], ALU.mult)
        e.tensor_tensor(krot, ksw, trig[:, pp * 512 + 256:pp * 512 + 512],
                        ALU.mult)
        e.drain()
        return e.tensor_tensor(krot, krot, t1, ALU.add)
    sc.op("vector", rope, waits=[(s_sw, s_sw.n), trg_tok], incs=[I(s_kc)])

    def split(e, pp=pp):
        e.tensor_copy(qvo[:, pp * 512:pp * 512 + 256], krot)
        e.drain()
        return e.tensor_tensor(qvo[:, pp * 512 + 256:pp * 512 + 512], krot,
                               qvo[:, pp * 512:pp * 512 + 256], ALU.subtract)
    w = [A["sto_tok"].get(idx - 2)]
    sc.op("vector", split, waits=w, incs=[I(s_stg)])
    sc.op("gpsimd", lambda e, oc=oc, slot=slot, pp=pp: e.dma_start(
        out=ctx["q_scr"][slot, oc, :, :],
        in_=qvo[:, pp * 512:pp * 512 + 512]),
        waits=[(s_stg, s_stg.n)], incs=[I(s_sto[pp], 16)])
    A["sto_tok"][idx] = (s_sto[pp], s_sto[pp].n)



def build_b(nc, sc, ctx):
    I = sc.inc
    st = ExitStack()

    def sbl(n, s, d):
        return st.enter_context(nc.sbuf_tensor(n, s, d))

    def psl(n, s, d):
        return st.enter_context(nc.psum_tensor(n, s, d))

    kvg = sbl("b_kvg", [128, 2 * NKB * 512], BF16)
    kvo = sbl("b_kvo", [128, 2 * 8 * 512], BF16)
    qsb = sbl("b_qsb", [128, 2 * 4 * 512], BF16)
    p32 = sbl("b_p32", [128, 8 * 256], F32)
    pbh = sbl("b_pbh", [128, 8 * 256], BF16)
    pbl = sbl("b_pbl", [128, 8 * 256], BF16)
    rr = sbl("b_rr", [1, 4 * 256], F32)
    bcv = sbl("b_bcv", [128, 256], F32)
    onrm = sbl("b_onrm", [128, 256], F32)
    ost = sbl("b_ost", [128, 2 * 512], BF16)

    ps = [psl(f"b_ps{i}", [128, 512], F32) for i in range(4)]
    po = [psl(f"b_po{i}", [128, 512], F32) for i in range(2)]
    pl = [psl(f"b_pl{i}", [128, 512], F32) for i in range(2)]

    s_kvl = [Sem(nc, "b_kvl0"), Sem(nc, "b_kvl1")]
    s_kol = [Sem(nc, "b_kol0"), Sem(nc, "b_kol1")]
    s_ql = [Sem(nc, "b_ql0"), Sem(nc, "b_ql1")]
    s_sc = Sem(nc, "b_sc")
    s_mk = Sem(nc, "b_mk")
    s_ex = Sem(nc, "b_ex")
    s_sp = Sem(nc, "b_sp")
    s_pv = Sem(nc, "b_pv")
    s_tl = Sem(nc, "b_tl")
    s_pe2 = Sem(nc, "b_pe2")
    s_sto = Sem(nc, "b_sto")

    B = dict(kvg=kvg, kvo=kvo, qsb=qsb, p32=p32, pbh=pbh, pbl=pbl,
             rr=rr, bcv=bcv,
             onrm=onrm, ost=ost, ps=ps, po=po, pl=pl, s_kvl=s_kvl,
             s_ql=s_ql, s_sc=s_sc, s_mk=s_mk, s_ex=s_ex, s_sp=s_sp,
             s_pv=s_pv, s_tl=s_tl, s_pe2=s_pe2, s_sto=s_sto)

    KVSEG = 8
    s_cc = ctx["s_cc"]

    def load_kv(g, buf, gate):
        wq = {0: [(s_cc, g + 1)] + ([gate] if gate else []),
              1: [(s_cc, g + 1)] + ([gate] if gate else [])}
        for k in range(KVSEG):
            nblk = NKB // KVSEG
            for bb in range(2):
                eng = "sync" if bb == 0 else "gpsimd"
                sc.op(eng, lambda e, g=g, k=k, buf=buf, nblk=nblk, bb=bb:
                      e.dma_start(
                          out=kvg[:, buf * NKB * 512 + k * nblk * 512:
                                  buf * NKB * 512 + (k + 1) * nblk * 512]
                          .rearrange("p (q f) -> p q f", f=512)[:, bb::2, :],
                          in_=ctx["kv_gath"][g, 4 * (k % 2):4 * (k % 2) + 4,
                                             k // 2, bb, :, :]
                          .rearrange("c p f -> p c f")), waits=wq[bb],
                      incs=[I(s_kvl[buf], 16)])
                wq[bb] = []
        return (s_kvl[buf], s_kvl[buf].n)

    def load_kvo(g, buf, gate):
        w = [(s_cc, 1)]
        if gate:
            w.append(gate)
        sc.op("sync", lambda e, g=g, buf=buf: e.dma_start(
            out=kvo[:, buf * 4096:(buf + 1) * 4096]
            .rearrange("p (j f) -> p j f", j=8),
            in_=ctx["kv_own"][g, :, :, :].rearrange("j p f -> p j f")),
            waits=w, incs=[I(s_kol[buf], 16)])
        return (s_kol[buf], s_kol[buf].n)

    B["kv_ld"] = [load_kv(0, 0, None)]
    B["kvo_ld"] = [load_kvo(0, 0, None)]
    qfree = {}
    pv_end_of_g = []
    for g in range(G):
        gb = g % 2
        for slot in range(NSLOT):
            sidx = g * NSLOT + slot
            qpp = sidx % 2
            w = [qfree.get(sidx - 2)]
            for h in range(4):
                sc.op("sync", lambda e, g=g, slot=slot, h=h, qpp=qpp:
                      e.dma_start(
                          out=qsb[:, qpp * 2048 + h * 512:
                                  qpp * 2048 + (h + 1) * 512],
                          in_=ctx["q_scr"][slot, g * 4 + h, :, :]),
                      waits=w if h == 0 else [], incs=[I(s_ql[qpp], 16)])
            B["ql_tok"] = (s_ql[qpp], s_ql[qpp].n)
            for wv in range(2):
                B["pend_tail"] = _b_wave(nc, sc, ctx, B, g, slot, wv)
            qfree[sidx] = (s_pv, s_pv.n)
        pv_end_of_g.append((s_pv, s_pv.n))
        if g + 1 < G:
            gate = pv_end_of_g[g - 1] if g >= 1 else None
            B["kv_ld"].append(load_kv(g + 1, 1 - gb, gate))
            B["kvo_ld"].append(load_kvo(g + 1, 1 - gb, gate))

    if B.get("pend_tail"):
        B["pend_tail"][0]()
        B["pend_tail"][1]()
    sc.op("sync", lambda e: e.nop(), waits=[(s_sto, s_sto.n)])
    sc.emit("B")
    st.close()


def _b_wave(nc, sc, ctx, B, g, slot, wv):
    I = sc.inc
    Kj = KS[slot]
    KjT = Kj + 2
    gb = g % 2
    sidx = g * NSLOT + slot
    qpp = sidx % 2
    kvg, qsb, p32, pbh, pbl = B["kvg"], B["qsb"], B["p32"], B["pbh"], B["pbl"]
    rr, bcv, onrm, ost = B["rr"], B["bcv"], B["onrm"], B["ost"]
    ps, po, pl = B["ps"], B["po"], B["pl"]
    s_kvl, s_ql, s_sc, s_mk, s_ex, s_sp, s_pv, s_tl, s_pe2, s_sto = (
        B["s_kvl"], B["s_ql"], B["s_sc"], B["s_mk"], B["s_ex"], B["s_sp"],
        B["s_pv"], B["s_tl"], B["s_pe2"], B["s_sto"])
    kv_ld = B["kv_ld"][g]
    kvo_ld = B["kvo_ld"][g]
    ql_tok = B["ql_tok"]
    assert isinstance(kv_ld, tuple) and isinstance(ql_tok, tuple)

    def kofs(it, which):
        off = {"kh": 0, "kl": 128, "vh": 256, "vl": 384}[which]
        if it < Kj:
            base = gb * (NKB * 512) + it * 512
            return kvg[:, base + off:base + off + 128]
        base = gb * 4096 + (6 - 2 * slot + (it - Kj)) * 512
        return B["kvo"][:, base + off:base + off + 128]

    def qofs(h, part):
        base = qpp * 2048 + h * 512 + (0 if part == "h" else 256)
        return qsb[:, base:base + 256]

    heads = (2 * wv, 2 * wv + 1)
    exp_tok = {}
    pv_tok = {}
    split_tok = {}
    exp_prev = (s_ex, B.get("exp_end", 0))
    if B.get("pend_tail"):
        B["pend_tail"][0]()

    def emit_pv(it):
        ib = it % 2

        def pvmm(e, it=it, Kj=KjT):
            r = None
            for hi in range(2):
                sl = slice((hi * 4 + it % 4) * 256,
                           (hi * 4 + it % 4 + 1) * 256)
                e.matmul(po[hi][:, 0:256], kofs(it, "vh"), pbh[:, sl],
                         start=(it == 0), stop=False)
                e.matmul(po[hi][:, 0:256], kofs(it, "vl"), pbh[:, sl],
                         start=False, stop=False)
                e.matmul(po[hi][:, 0:256], kofs(it, "vh"), pbl[:, sl],
                         start=False, stop=(it == Kj - 1))
                e.matmul(pl[hi][0:1, 0:256], ctx["c_ones"][:, :], pbh[:, sl],
                         start=(it == 0), stop=False)
                r = e.matmul(pl[hi][0:1, 0:256], ctx["c_ones"][:, :],
                             pbl[:, sl], start=False, stop=(it == Kj - 1))
            return r
        w = [(s_sp, split_tok[it])]
        if it == 0 and B.get("tail_tok"):
            w.append(B["tail_tok"])
        sc.op("tensor", pvmm, waits=w, incs=[I(s_pv)])
        pv_tok[it] = s_pv.n

    for it in range(KjT):
        ib = it % 2

        def smm(e, it=it, ib=ib):
            r = None
            for hi, h in enumerate(heads):
                bank = ps[hi * 2 + ib]
                e.matmul(bank[:, 0:256], kofs(it, "kh"), qofs(h, "h"),
                         start=True, stop=False)
                e.matmul(bank[:, 0:256], kofs(it, "kl"), qofs(h, "h"),
                         start=False, stop=False)
                r = e.matmul(bank[:, 0:256], kofs(it, "kh"), qofs(h, "l"),
                             start=False, stop=True)
            return r
        w = [kv_ld, ql_tok]
        if it >= Kj:
            w.append(kvo_ld)
        if it >= 2:
            w.append((s_ex, exp_tok[it - 2]))
        else:
            w.append(exp_prev)
        sc.op("tensor", smm, waits=w, incs=[I(s_sc)])

        diag = it >= Kj
        if diag:
            mask = ctx["c_maskA"] if it == Kj else ctx["c_maskB"]

            def madd(e, ib=ib, mask=mask):
                r = None
                for hi in range(2):
                    bank = ps[hi * 2 + ib]
                    r = e.tensor_tensor(bank[:, 0:256], bank[:, 0:256],
                                        mask[:, :], ALU.add)
                return r
            sc.op("vector", madd, waits=[(s_sc, s_sc.n)], incs=[I(s_mk)])

        def expf(e, it=it, ib=ib, slot=slot):
            r = None
            for hi in range(2):
                bank = ps[hi * 2 + ib]
                sl4 = (hi * 4 + it % 4) * 256
                r = e.activation(
                    p32[:, sl4:sl4 + 256],
                    bank[:, 0:256], AF.Exp,
                    bias=ctx["c_bias"][:, slot * (NKB + 2) + it:
                                       slot * (NKB + 2) + it + 1],
                    scale=DH ** -0.5)
            return r
        w = [(s_sc, s_sc.n)]
        if diag:
            w.append((s_mk, s_mk.n))
        sc.op("scalar", expf, waits=w, incs=[I(s_ex)])
        exp_tok[it] = s_ex.n

        def spl(e, it=it):
            for hi in range(2):
                sl = slice((hi * 4 + it % 4) * 256,
                           (hi * 4 + it % 4 + 1) * 256)
                e.tensor_copy(pbh[:, sl], p32[:, sl])
            e.drain()
            r = None
            for hi in range(2):
                sl = slice((hi * 4 + it % 4) * 256,
                           (hi * 4 + it % 4 + 1) * 256)
                r = e.tensor_tensor(pbl[:, sl], p32[:, sl], pbh[:, sl],
                                    ALU.subtract)
            return r
        w = [(s_ex, s_ex.n)]
        if it >= 4:
            w.append((s_pv, pv_tok[it - 4]))
        sc.op("vector", spl, waits=w, incs=[I(s_sp)])
        split_tok[it] = s_sp.n
        if it == 1 and B.get("pend_tail"):
            B["pend_tail"][1]()
            B["pend_tail"] = None
            B["tail_tok"] = B["po_free"]
        if it >= 3:
            emit_pv(it - 3)
    emit_pv(KjT - 3)
    emit_pv(KjT - 2)
    emit_pv(KjT - 1)

    B["exp_end"] = s_ex.n

    def tail_stage1(pv_end=(s_pv, s_pv.n)):
        def recips(e):
            e.reciprocal(rr[0:1, 0:256], pl[0][0:1, 0:256])
            return e.reciprocal(rr[0:1, 512:768], pl[1][0:1, 0:256])
        sc.op("vector", recips, waits=[pv_end], incs=[I(s_tl)])
        B["rc_tok"] = (s_tl, s_tl.n)

    def tail_stage2(heads=heads, g=g, slot=slot):
        rc_tok = B["rc_tok"]
        for hi in range(2):
            sc.op("tensor", lambda e, hi=hi: e.matmul(
                po[hi][:, 256:512], ctx["c_onesrf"][:, :],
                rr[0:1, hi * 512:hi * 512 + 256],
                start=True, stop=True), waits=[rc_tok], incs=[I(s_pe2)])
        onrm_q = []
        for hi, h in enumerate(heads):
            oc = g * 4 + h
            sc.op("vector", lambda e, hi=hi: e.tensor_copy(
                bcv[:, :], po[hi][:, 256:512]),
                waits=[(s_pe2, s_pe2.n - (1 - hi))], incs=[I(s_tl)])
            sc.op("vector", lambda e, hi=hi: e.tensor_tensor(
                onrm[:, :], po[hi][:, 0:256], bcv[:, :], ALU.mult),
                incs=[I(s_tl)])
            B["po_free"] = (s_tl, s_tl.n)

            def osplit(e, hi=hi):
                e.tensor_copy(ost[:, hi * 512:hi * 512 + 256], onrm[:, :])
                e.drain()
                return e.tensor_tensor(ost[:, hi * 512 + 256:hi * 512 + 512],
                                       onrm[:, :],
                                       ost[:, hi * 512:hi * 512 + 256],
                                       ALU.subtract)
            sc.op("vector", osplit, waits=[(s_sto, s_sto.n)], incs=[I(s_tl)])
            sc.op("gpsimd", lambda e, oc=oc, slot=slot, hi=hi: e.dma_start(
                out=ctx["o_scr"][oc, slot, :, :],
                in_=ost[:, hi * 512:hi * 512 + 512]),
                waits=[(s_tl, s_tl.n)], incs=[I(s_sto, 16)])
    return (tail_stage1, tail_stage2)


def build_c(nc, sc, ctx):
    I = sc.inc
    st = ExitStack()

    def sbl(n, s, d):
        return st.enter_context(nc.sbuf_tensor(n, s, d))

    def psl(n, s, d):
        return st.enter_context(nc.psum_tensor(n, s, d))

    woh = sbl("c_woh", [128, HC * 1024], BF16)
    wol = sbl("c_wol", [128, HC * 1024], BF16)
    wldf = sbl("c_wldf", [128, 2 * 1024], F32)
    osb = sbl("c_osb", [128, 2 * HC * 512], BF16)
    xov = sbl("c_xov", [128, 3 * 256], F32)
    x2f = sbl("c_x2f", [128, 3 * 256], F32)
    x2p = sbl("c_x2p", [128, 3 * 256], F32)

    pc = [psl(f"c_pc{i}", [128, 512], F32) for i in range(3)]
    pT = [psl(f"c_pT{i}", [128, 512], F32) for i in range(3)]

    s_ldw = [Sem(nc, "c_ldw0"), Sem(nc, "c_ldw1")]
    s_wsp = Sem(nc, "c_wsp")
    s_ol = [Sem(nc, "c_ol0"), Sem(nc, "c_ol1")]
    s_xl = [Sem(nc, f"c_xl{i}") for i in range(3)]
    s_mm = Sem(nc, "c_mm")
    s_ev = Sem(nc, "c_ev")
    s_tr = Sem(nc, "c_tr")
    s_ad = Sem(nc, "c_ad")
    s_sto = [Sem(nc, f"c_sto{i}") for i in range(3)]

    ofree = {}
    idx = 0
    ev_tokd = {}
    tr_tokd = {}
    ad_tokd = {}
    sto_tokd = {}
    pend = []

    def emit_cpost(idx, oc, slot, epp):
        def trp(e, epp=epp):
            r = None
            for b in range(2):
                r = e.transpose(pT[epp][:, b * 128:(b + 1) * 128],
                                x2f[:, epp * 256 + b * 128:
                                    epp * 256 + (b + 1) * 128],
                                ctx["c_idf"][:, :])
            return r
        w = [(s_ev, ev_tokd[idx])]
        if idx >= 3:
            w.append((s_ad, ad_tokd[idx - 3]))
        sc.op("tensor", trp, waits=w, incs=[I(s_tr)])
        tr_tokd[idx] = s_tr.n

        for b in range(2):
            tl = slot * 2 + b
            w = []
            if b == 0 and idx >= 3:
                w.append((s_ad, ad_tokd[idx - 3]))
            sc.op("sync", lambda e, tl=tl, oc=oc, epp=epp, b=b:
                  e.dma_start(
                      out=xov[:, epp * 256 + b * 128:
                              epp * 256 + (b + 1) * 128],
                      in_=ctx["xown"][tl, :, oc * 128:(oc + 1) * 128]),
                  waits=w, incs=[I(s_xl[epp], 16)])

        def radd(e, epp=epp):
            r = None
            for b in range(2):
                r = e.tensor_tensor(
                    x2p[:, epp * 256 + b * 128:epp * 256 + (b + 1) * 128],
                    pT[epp][:, b * 128:(b + 1) * 128],
                    xov[:, epp * 256 + b * 128:epp * 256 + (b + 1) * 128],
                    ALU.add)
            return r
        w = [(s_tr, tr_tokd[idx]), (s_xl[epp], s_xl[epp].n)]
        if idx >= 3:
            w.append(sto_tokd[idx - 3])
        sc.op("vector", radd, waits=w, incs=[I(s_ad)])
        ad_tokd[idx] = s_ad.n
        for b in range(2):
            tl = slot * 2 + b
            sc.op("gpsimd", lambda e, tl=tl, oc=oc, epp=epp, b=b:
                  e.dma_start(
                      out=ctx["x2_scr"][tl, :, oc * 128:(oc + 1) * 128],
                      in_=x2p[:, epp * 256 + b * 128:
                              epp * 256 + (b + 1) * 128]),
                  waits=[(s_ad, s_ad.n)] if b == 0 else [],
                  incs=[I(s_sto[epp], 16)])
        sto_tokd[idx] = (s_sto[epp], s_sto[epp].n)

    for half in range(2):
        wldf_free = {}
        half_gate = (s_mm, s_mm.n) if half else None
        for dc in range(HC):
            b = dc % 2
            w = [wldf_free.get(dc - 2)]
            if half_gate:
                w.append(half_gate)
            sc.op("sync", lambda e, dc=dc, b=b, half=half: e.dma_start(
                out=wldf[:, b * 1024:(b + 1) * 1024],
                in_=ctx["wo"][dc * 128:(dc + 1) * 128,
                              half * 1024:(half + 1) * 1024]), waits=w,
                incs=[I(s_ldw[b], 16)])
            sc.op("vector", lambda e, dc=dc, b=b: e.tensor_copy(
                woh[:, dc * 1024:(dc + 1) * 1024],
                wldf[:, b * 1024:(b + 1) * 1024]),
                waits=[(s_ldw[b], s_ldw[b].n)], incs=[I(s_wsp)])
            sc.op("vector", lambda e, dc=dc, b=b: e.tensor_tensor(
                wol[:, dc * 1024:(dc + 1) * 1024],
                wldf[:, b * 1024:(b + 1) * 1024],
                woh[:, dc * 1024:(dc + 1) * 1024], ALU.subtract),
                incs=[I(s_wsp)])
            wldf_free[dc] = (s_wsp, s_wsp.n)
        wsp_half = s_wsp.n

        for slot in range(NSLOT):
            sl_i = half * NSLOT + slot
            spp = slot % 2
            w = [ofree.get(sl_i - 2)]
            for dc in range(HC):
                sc.op("sync", lambda e, slot=slot, dc=dc, spp=spp:
                      e.dma_start(
                          out=osb[:, spp * HC * 512 + dc * 512:
                                  spp * HC * 512 + (dc + 1) * 512],
                          in_=ctx["o_scr"][dc, slot, :, :]),
                      waits=w if dc == 0 else [], incs=[I(s_ol[spp], 16)])
            ol_tok = (s_ol[spp], s_ol[spp].n)
            for oci in range(8):
                oc = half * 8 + oci
                bank = pc[idx % 3]
                epp = idx % 3

                def cmm(e, oci=oci, spp=spp, bank=bank):
                    r = None
                    for dc in range(HC):
                        ob = spp * HC * 512 + dc * 512
                        ws = slice(dc * 1024 + oci * 128,
                                   dc * 1024 + (oci + 1) * 128)
                        e.matmul(bank[:, 0:256], woh[:, ws],
                                 osb[:, ob:ob + 256], start=(dc == 0),
                                 stop=False)
                        e.matmul(bank[:, 0:256], wol[:, ws],
                                 osb[:, ob:ob + 256], start=False,
                                 stop=False)
                        r = e.matmul(bank[:, 0:256], woh[:, ws],
                                     osb[:, ob + 256:ob + 512], start=False,
                                     stop=(dc == HC - 1))
                    return r
                w = [ol_tok, (s_wsp, wsp_half)]
                if idx >= 3:
                    w.append((s_ev, ev_tokd[idx - 3]))
                sc.op("tensor", cmm, waits=w, incs=[I(s_mm)])
                mm_tok = s_mm.n

                w = [(s_mm, mm_tok)]
                if idx >= 3:
                    w.append((s_tr, tr_tokd[idx - 3]))
                sc.op("scalar", lambda e, epp=epp, bank=bank: e.activation(
                    x2f[:, epp * 256:(epp + 1) * 256], bank[:, 0:256],
                    AF.Copy, bias=0.0, scale=1.0), waits=w, incs=[I(s_ev)])
                ev_tokd[idx] = s_ev.n
                pend.append((idx, oc, slot, epp))
                if len(pend) >= 2:
                    emit_cpost(*pend.pop(0))
                idx += 1
            ofree[sl_i] = (s_mm, s_mm.n)
    while pend:
        emit_cpost(*pend.pop(0))

    sc.op("sync", lambda e: e.nop(),
          waits=[(s_sto[0], s_sto[0].n), (s_sto[1], s_sto[1].n),
                 (s_sto[2], s_sto[2].n)])
    sc.emit("C")
    st.close()


def build_d(nc, sc, ctx):
    I = sc.inc
    st = ExitStack()

    def sbl(n, s, d):
        return st.enter_context(nc.sbuf_tensor(n, s, d))

    def psl(n, s, d):
        return st.enter_context(nc.psum_tensor(n, s, d))

    NH2 = NIC // 2
    hh = sbl("d_hh", [128, NH2 * 512], BF16)
    hl = sbl("d_hl", [128, NH2 * 512], BF16)
    dpart = sbl("d_dpart", [128, HC * 512], F32)
    x2h = sbl("d_x2h", [128, HC * 512], BF16)
    x2l = sbl("d_x2l", [128, HC * 512], BF16)
    x2t = None
    bns = sbl("d_bns", [128, 24], F32)
    mv = sbl("d_mv", [128, 2], F32)
    stt = sbl("d_stt", [128, 4], F32)
    lnc = sbl("d_lnc", [128, D], F32)
    lnh = sbl("d_lnh", [128, D], BF16)
    lnl = sbl("d_lnl", [128, D], BF16)
    wbuf = sbl("d_wbuf", [128, 16384], BF16)
    hrow = sbl("d_hrow", [1, 2 * 512], BF16)
    hrf = sbl("d_hrf", [1, 512], F32)
    gsl = sbl("d_gsl", [128, 2 * 512], F32)
    h32 = sbl("d_h32", [128, 512], F32)
    dev = gsl[:, 0:512]
    otm = lnc[:, 0:1024]

    pgu = [psl(f"d_pgu{i}", [128, 512], F32) for i in range(4)]
    pwT = [psl(f"d_pwT{i}", [128, 1024], BF16) for i in range(3)]
    pH = psl("d_pH", [128, 512], F32)
    pdn = pgu

    s_x2 = Sem(nc, "d_x2")
    hop = Sem(nc, "d_hop")
    s_ln = Sem(nc, "d_ln")
    s_sp = Sem(nc, "d_sp")
    s_xT = Sem(nc, "d_xT")
    s_xe = Sem(nc, "d_xe")
    s_wl = [Sem(nc, "d_wl0"), Sem(nc, "d_wl1")]
    s_wT = Sem(nc, "d_wT")
    s_we = Sem(nc, "d_we")
    s_gu = Sem(nc, "d_gu")
    s_si = Sem(nc, "d_si")
    s_hs = Sem(nc, "d_hs")
    s_H = Sem(nc, "d_H")
    s_dn = Sem(nc, "d_dn")
    s_de = Sem(nc, "d_de")
    s_de2 = Sem(nc, "d_de2")
    s_oT = Sem(nc, "d_oT")
    s_oa = Sem(nc, "d_oa")
    s_sto = Sem(nc, "d_sto")
    s_x2r = Sem(nc, "d_x2r")

    tbank = {"i": 0}
    lastev = {}

    def next_tb():
        b = tbank["i"] % 3
        tbank["i"] += 1
        return b

    sc.op("scalar", lambda e: e.nop(), waits=[ctx["scl_tok"]])

    W_DS, W_DT = 0, 4 * D

    pdn_free = {}
    si_free = {}
    pair_gate = None
    pair_gate2 = None
    for pair in range(2):
        xT_tok_prev = None
        for tl in range(4):
            gtl = pair * 4 + tl
            w = [pair_gate] if (pair and tl == 0) else []
            if tl >= 1:
                w = [(s_sp, s_sp.n)]
            w = []
            if pair and tl == 0:
                w.append(pair_gate)
                w.append(pair_gate2)
            if tl >= 1 or pair:
                w.append((s_ln, s_ln.n))
            sc.op("sync", lambda e, gtl=gtl: e.dma_start(
                out=dpart[:, 0:D], in_=ctx["x2_scr"][gtl, :, :]), waits=w,
                incs=[I(s_x2, 16)])
            ap_w = [(s_sp, s_sp.n)]
            ln_tile(sc, nc, dpart[:, 0:D], bns[:, :], mv[:, :], stt[:, :],
                    lnc[:, :], EPS_LN, [(s_x2, s_x2.n)], ap_w, [I(s_ln)],
                    hop)

            def spl(e):
                e.tensor_copy(lnh[:, :], lnc[:, :])
                e.drain()
                return e.tensor_tensor(lnl[:, :], lnc[:, :], lnh[:, :],
                                       ALU.subtract)
            w = [(s_ln, s_ln.n)]
            if xT_tok_prev is not None:
                w.append(xT_tok_prev)
            sc.op("vector", spl, waits=w, incs=[I(s_sp)])
            for r in range(8):
                tb = next_tb()

                def trx(e, r=r, tb=tb):
                    rr_ = None
                    for k in range(2):
                        hc = 2 * r + k
                        rr_ = e.transpose(
                            pwT[tb][:, k * 256:k * 256 + 128],
                            lnh[:, hc * 128:(hc + 1) * 128],
                            ctx["c_idb"][:, :])
                        rr_ = e.transpose(
                            pwT[tb][:, k * 256 + 128:k * 256 + 256],
                            lnl[:, hc * 128:(hc + 1) * 128],
                            ctx["c_idb"][:, :])
                    return rr_
                w = [(s_sp, s_sp.n), lastev.get(tb)]
                sc.op("tensor", trx, waits=w, incs=[I(s_xT)])

                def xev(e, r=r, tl=tl, tb=tb):
                    rr_ = None
                    for k in range(2):
                        hc = 2 * r + k
                        dst = hc * 512 + tl * 128
                        rr_ = e.tensor_copy(
                            x2h[:, dst:dst + 128],
                            pwT[tb][:, k * 256:k * 256 + 128])
                        rr_ = e.tensor_copy(
                            x2l[:, dst:dst + 128],
                            pwT[tb][:, k * 256 + 128:k * 256 + 256])
                    return rr_
                sc.op("vector", xev, waits=[(s_xT, s_xT.n)], incs=[I(s_xe)])
                lastev[tb] = (s_xe, s_xe.n)
            xT_tok_prev = (s_xT, s_xT.n)
        xe_done = s_xe.n

        wst_free = {}
        we_toks = {}
        hacc_pend = []

        def flush_hacc():
            while hacc_pend:
                fn, tok = hacc_pend.pop(0)
                sc.op("tensor", fn, waits=[(s_hs, tok)], incs=[I(s_H)])

        def emit_wload(ic, boundary):
            W_GS = (ic % 2) * 2 * D
            w = [wst_free.get(ic - 2)]
            if boundary:
                w.append((s_wT, s_wT.n))
            sc.op("gpsimd", lambda e, ic=ic, W_GS=W_GS: e.dma_start(
                out=wbuf[:, W_GS:W_GS + D],
                in_=ctx["gw"][ic * 128:(ic + 1) * 128, :]), waits=w,
                incs=[I(s_wl[ic % 2], 16)])
            sc.op("gpsimd", lambda e, ic=ic, W_GS=W_GS: e.dma_start(
                out=wbuf[:, W_GS + D:W_GS + 2 * D],
                in_=ctx["uw"][ic * 128:(ic + 1) * 128, :]),
                incs=[I(s_wl[ic % 2], 16)])

        def emit_wtrans(ic):
            wl_tok = (s_wl[ic % 2], s_wl[ic % 2].n)
            W_GS = (ic % 2) * 2 * D
            GT = 4 * D + (ic % 2) * 2 * D
            for part in range(2):
                src_off = W_GS + part * D
                dst_off = GT + part * D
                for r in range(4):
                    tb = next_tb()

                    def trw(e, part=part, r=r, src_off=src_off, tb=tb):
                        rr_ = None
                        for k in range(4):
                            hc = 4 * r + k
                            rr_ = e.transpose(
                                pwT[tb][:, k * 128:(k + 1) * 128],
                                wbuf[:, src_off + hc * 128:
                                     src_off + (hc + 1) * 128],
                                ctx["c_idb"][:, :])
                        return rr_
                    w = [wl_tok, (s_xe, xe_done), lastev.get(tb)]
                    sc.op("tensor", trw, waits=w, incs=[I(s_wT)])
                    sc.op("vector", lambda e, r=r, dst_off=dst_off, tb=tb:
                          e.tensor_copy(
                              wbuf[:, dst_off + r * 512:
                                   dst_off + (r + 1) * 512],
                              pwT[tb][:, 0:512]),
                          waits=[(s_wT, s_wT.n)], incs=[I(s_we)])
                    lastev[tb] = (s_we, s_we.n)
            wst_free[ic] = (s_wT, s_wT.n)
            we_toks[ic] = (s_we, s_we.n)

        for icb in range(2):
            first = icb * NH2
            emit_wload(first, True)
            emit_wtrans(first)
            for ici in range(NH2):
                ic = icb * NH2 + ici
                if ici + 1 < NH2:
                    emit_wload(ic + 1, False)

                gp = 2 * (ic % 2)
                GT = 4 * D + (ic % 2) * 2 * D

                def gumm(e, gp=gp, GT=GT):
                    r = None
                    for part, bank in ((0, pgu[gp]), (1, pgu[gp + 1])):
                        woff = GT + part * D
                        for hc in range(HC):
                            ws = slice(woff + hc * 128,
                                       woff + (hc + 1) * 128)
                            e.matmul(bank[:, 0:512], wbuf[:, ws],
                                     x2h[:, hc * 512:(hc + 1) * 512],
                                     start=(hc == 0), stop=False)
                            r = e.matmul(bank[:, 0:512], wbuf[:, ws],
                                         x2l[:, hc * 512:(hc + 1) * 512],
                                         start=False, stop=(hc == HC - 1))
                    return r
                w = [(s_xe, xe_done), we_toks[ic]]
                w += si_free.get(ic - 2, [])
                if ici == 0:
                    w.append((s_dn, s_dn.n))
                sc.op("tensor", gumm, waits=w, incs=[I(s_gu)])
                gu_tok = s_gu.n
                if ici + 1 < NH2:
                    emit_wtrans(ic + 1)

                w = [(s_gu, gu_tok), (s_hs, s_hs.n)]
                sc.op("scalar", lambda e, ic=ic, gp=gp: e.activation(
                    gsl[:, 0:512], pgu[gp][:, 0:512], AF.Silu,
                    scale=ctx["c_sg"][:, ic:ic + 1]), waits=w,
                    incs=[I(s_si)])
                sc.op("scalar", lambda e, ic=ic, gp=gp: e.activation(
                    gsl[:, 512:1024], pgu[gp + 1][:, 0:512], AF.Identity,
                    scale=ctx["c_su"][:, ic:ic + 1]),
                    waits=[(s_gu, gu_tok)], incs=[I(s_si)])
                si_free[ic] = [(s_si, s_si.n)]

                def hmul(e, ic=ic, ici=ici):
                    e.tensor_tensor(h32[:, :], gsl[:, 0:512],
                                    gsl[:, 512:1024], ALU.mult)
                    e.drain()
                    e.tensor_copy(hh[:, ici * 512:(ici + 1) * 512],
                                  h32[:, :])
                    e.drain()
                    return e.tensor_tensor(hl[:, ici * 512:(ici + 1) * 512],
                                           h32[:, :],
                                           hh[:, ici * 512:(ici + 1) * 512],
                                           ALU.subtract)
                sc.op("vector", hmul, waits=[(s_si, s_si.n)], incs=[I(s_hs)])

                def hacc(e, ic=ic, ici=ici):
                    e.matmul(pH[0:1, 0:512], ctx["c_ones"][:, :],
                             hh[:, ici * 512:(ici + 1) * 512],
                             start=(ic == 0), stop=False)
                    return e.matmul(pH[0:1, 0:512], ctx["c_ones"][:, :],
                                    hl[:, ici * 512:(ici + 1) * 512],
                                    start=False, stop=(ic == NIC - 1))
                hacc_pend.append((hacc, s_hs.n))
                if len(hacc_pend) > 1:
                    fn, tok = hacc_pend.pop(0)
                    sc.op("tensor", fn, waits=[(s_hs, tok)], incs=[I(s_H)])
            flush_hacc()
            gu_half_done = s_hs.n

            if icb == 1:
                def hsplit(e):
                    e.tensor_copy(hrf[0:1, :], pH[0:1, 0:512])
                    e.drain()
                    e.tensor_copy(hrow[0:1, 0:512], hrf[0:1, :])
                    e.drain()
                    return e.tensor_tensor(hrow[0:1, 512:1024], hrf[0:1, :],
                                           hrow[0:1, 0:512], ALU.subtract)
                sc.op("vector", hsplit, waits=[(s_H, s_H.n)], incs=[I(s_hs)])
            hrow_tok = s_hs.n

            wl_toks = {}
            we_toks = {}

            def emit_dload(hc):
                w = [(s_wT, s_wT.n)]
                sc.op("gpsimd", lambda e, hc=hc, icb=icb: e.dma_start(
                    out=wbuf[:, W_DS:W_DS + NH2 * 128],
                    in_=ctx["dw"][hc * 128:(hc + 1) * 128,
                                  icb * (NH2 * 128):
                                  (icb + 1) * (NH2 * 128)]),
                    waits=w, incs=[I(s_wl[hc % 2], 16)])
                wl_toks[hc] = (s_wl[hc % 2], s_wl[hc % 2].n)

            def emit_trd(hc):
                for r in range(8):
                    tb = next_tb()

                    def trd(e, r=r, tb=tb):
                        rr_ = None
                        for k in range(4):
                            ici = 4 * r + k
                            rr_ = e.transpose(
                                pwT[tb][:, k * 128:(k + 1) * 128],
                                wbuf[:, W_DS + ici * 128:
                                     W_DS + (ici + 1) * 128],
                                ctx["c_idb"][:, :])
                        return rr_
                    w = [wl_toks[hc], lastev.get(tb)]
                    sc.op("tensor", trd, waits=w, incs=[I(s_wT)])
                    sc.op("vector", lambda e, r=r, tb=tb: e.tensor_copy(
                        wbuf[:, W_DT + r * 512:W_DT + (r + 1) * 512],
                        pwT[tb][:, 0:512]),
                        waits=[(s_wT, s_wT.n)], incs=[I(s_we)])
                    lastev[tb] = (s_we, s_we.n)
                we_toks[hc] = s_we.n

            def emit_dmm(hc):
                dpp = hc % 2

                def dmm(e, hc=hc, dpp=dpp, icb=icb):
                    bank = pdn[dpp]
                    first = True
                    if icb == 1:
                        e.matmul(bank[:, 0:512], ctx["c_n128"][:, :],
                                 hrow[0:1, 0:512], start=True, stop=False)
                        e.matmul(bank[:, 0:512], ctx["c_n128"][:, :],
                                 hrow[0:1, 512:1024], start=False,
                                 stop=False)
                        first = False
                    r = None
                    for ici in range(NH2):
                        ws = slice(W_DT + ici * 128, W_DT + (ici + 1) * 128)
                        e.matmul(bank[:, 0:512], wbuf[:, ws],
                                 hh[:, ici * 512:(ici + 1) * 512],
                                 start=first, stop=False)
                        first = False
                        r = e.matmul(bank[:, 0:512], wbuf[:, ws],
                                     hl[:, ici * 512:(ici + 1) * 512],
                                     start=False, stop=(ici == NH2 - 1))
                    return r
                ghc = (pair * 2 + icb) * HC + hc
                w = [(s_we, we_toks[hc]), (s_hs, hrow_tok)]
                w += pdn_free.get(ghc - 2, [])
                sc.op("tensor", dmm, waits=w, incs=[I(s_dn)])
                dn_toks[hc] = s_dn.n

            def emit_stash(hc):
                dpp = hc % 2
                ghc = (pair * 2 + icb) * HC + hc
                sc.op("vector", lambda e, hc=hc, dpp=dpp: e.tensor_copy(
                    dpart[:, hc * 512:(hc + 1) * 512],
                    pdn[dpp][:, 0:512]), waits=[(s_dn, dn_toks[hc])],
                    incs=[I(s_de)])
                pdn_free[ghc] = [(s_de, s_de.n)]

            def emit_tail(hc):
                dpp = hc % 2
                ghc = (pair * 2 + icb) * HC + hc
                w = [(s_dn, dn_toks[hc])]
                if s_oT.n >= 1:
                    w.append((s_oT, s_oT.n))
                sc.op("vector", lambda e, hc=hc, dpp=dpp: e.tensor_tensor(
                    dev[:, :], pdn[dpp][:, 0:512],
                    dpart[:, hc * 512:(hc + 1) * 512], ALU.add), waits=w,
                    incs=[I(s_de)])
                pdn_free[ghc] = [(s_de, s_de.n)]
                sc.op("scalar", lambda e, hc=hc: e.activation(
                    dev[:, :], dev[:, :], AF.Copy, bias=0.0,
                    scale=ctx["c_sd"][:, hc:hc + 1]),
                    waits=[(s_de, s_de.n)], incs=[I(s_de2)])
                de_tok = s_de2.n

                def dtr(e, hc=hc):
                    rr_ = None
                    for b in range(4):
                        rr_ = e.transpose(
                            pgu[2 + b % 2][:, (b // 2) * 128:
                                           (b // 2) * 128 + 128],
                            dev[:, b * 128:(b + 1) * 128],
                            ctx["c_idf"][:, :])
                    return rr_
                w = [(s_de2, de_tok)]
                if s_oT.n >= 1:
                    w.append((s_oa, s_oa.n))
                sc.op("tensor", dtr, waits=w, incs=[I(s_oT)])
                for b in range(4):
                    gtl = pair * 4 + b
                    w = [(s_oa, s_oa.n), (s_sp, s_sp.n)] if b == 0 else []
                    sc.op("sync", lambda e, gtl=gtl, hc=hc, b=b: e.dma_start(
                        out=otm[:, 512 + b * 128:512 + (b + 1) * 128],
                        in_=ctx["x2_scr"][gtl, :, hc * 128:(hc + 1) * 128]),
                        waits=w, incs=[I(s_x2r, 16)])

                def oadd(e, hc=hc):
                    rr_ = None
                    for b in range(4):
                        rr_ = e.tensor_tensor(
                            otm[:, b * 128:(b + 1) * 128],
                            pgu[2 + b % 2][:, (b // 2) * 128:
                                           (b // 2) * 128 + 128],
                            otm[:, 512 + b * 128:512 + (b + 1) * 128],
                            ALU.add)
                    return rr_
                sc.op("vector", oadd,
                      waits=[(s_oT, s_oT.n), (s_x2r, s_x2r.n),
                             (s_sto, s_sto.n)], incs=[I(s_oa)])
                for b in range(4):
                    gtl = pair * 4 + b
                    sc.op("gpsimd", lambda e, gtl=gtl, hc=hc, b=b:
                          e.dma_start(
                              out=ctx["out"][gtl, :,
                                             hc * 128:(hc + 1) * 128],
                              in_=otm[:, b * 128:(b + 1) * 128]),
                          waits=[(s_oa, s_oa.n)] if b == 0 else [],
                          incs=[I(s_sto, 16)])

            dn_toks = {}
            emit_dload(0)
            emit_trd(0)
            tail_hc = None
            for hc in range(HC):
                if hc + 1 < HC:
                    emit_dload(hc + 1)
                emit_dmm(hc)
                if hc + 1 < HC:
                    emit_trd(hc + 1)
                if icb == 0:
                    emit_stash(hc)
                else:
                    if tail_hc is not None:
                        emit_tail(tail_hc)
                    tail_hc = hc
            if icb == 1:
                emit_tail(tail_hc)

        pair_gate = (s_oa, s_oa.n)
        pair_gate2 = (s_sto, s_sto.n)

    sc.op("sync", lambda e: e.nop(), waits=[(s_sto, s_sto.n)])
    sc.op("gpsimd", lambda e: e.nop(), waits=[(s_sto, s_sto.n)])
    sc.emit("D")
    st.close()


_NC_CACHE = {}


def _get_nc(dbg=DBG, phases=("A1", "A2", "B", "C", "D")):
    key = (dbg, tuple(phases))
    if key not in _NC_CACHE:
        _NC_CACHE[key] = build_nc(dbg, phases)
    return _NC_CACHE[key]


def make_inmaps(inputs):
    cos_t, sin_t, maskA, maskB = host_tables()
    id_f = np.eye(128, dtype=np.float32)
    id_b = np.eye(128, dtype=BFNP)
    x = np.ascontiguousarray(
        np.asarray(inputs["L0_attn_x"], dtype=np.float32).reshape(T, D))
    gq = np.ascontiguousarray(
        np.asarray(inputs["L0_attn_gq"], np.float32).reshape(-1))
    gk = np.asarray(inputs["L0_attn_gk"], np.float32)
    gk_rep = np.ascontiguousarray(np.repeat(gk, 4, axis=0).reshape(-1))
    shared = {
        "L0_attn_wq": np.ascontiguousarray(
            np.asarray(inputs["L0_attn_wq"], np.float32)),
        "L0_attn_wk": np.ascontiguousarray(
            np.asarray(inputs["L0_attn_wk"], np.float32)),
        "L0_attn_wv": np.ascontiguousarray(
            np.asarray(inputs["L0_attn_wv"], np.float32)),
        "L0_attn_wo": np.ascontiguousarray(
            np.asarray(inputs["L0_attn_wo"], np.float32)),
        "L0_ffn_gate_w": np.ascontiguousarray(
            np.asarray(inputs["L0_ffn_gate_w"], np.int32)),
        "L0_ffn_up_w": np.ascontiguousarray(
            np.asarray(inputs["L0_ffn_up_w"], np.int32)),
        "L0_ffn_down_w": np.ascontiguousarray(
            np.asarray(inputs["L0_ffn_down_w"], np.int32)),
        "L0_ffn_gate_scale": np.ascontiguousarray(
            np.asarray(inputs["L0_ffn_gate_scale"], np.float32)),
        "L0_ffn_up_scale": np.ascontiguousarray(
            np.asarray(inputs["L0_ffn_up_scale"], np.float32)),
        "L0_ffn_down_scale": np.ascontiguousarray(
            np.asarray(inputs["L0_ffn_down_scale"], np.float32)),
        "gq_flat": gq, "gk_flat": np.ascontiguousarray(gk.reshape(-1)),
        "gk_rep": gk_rep, "maskA": maskA, "maskB": maskB,
        "id_f32": id_f, "id_bf16": id_b,
    }
    in_maps = []
    plans = []
    for c in range(NCORE):
        slots, prog = plan_core(c)
        plans.append((slots, prog))
        xo = np.zeros((2 * NSLOT, KB, D), np.float32)
        co = np.zeros((128, NSLOT * QB), np.float32)
        so = np.zeros((128, NSLOT * QB), np.float32)
        for j, b in enumerate(slots):
            xo[2 * j] = x[b * QB:b * QB + KB]
            xo[2 * j + 1] = x[b * QB + KB:(b + 1) * QB]
            co[:, j * QB:(j + 1) * QB] = cos_t[:, b * QB:(b + 1) * QB]
            so[:, j * QB:(j + 1) * QB] = sin_t[:, b * QB:(b + 1) * QB]
        pad = np.zeros((NSLOT, NKB + 2), np.float32)
        for j, b in enumerate(slots):
            for i in range(KS[j]):
                if i >= 2 * b:
                    pad[j, i] = 1.0
        padrep = np.ascontiguousarray(
            np.broadcast_to(pad.reshape(1, -1),
                            (128, NSLOT * (NKB + 2))))
        m = dict(shared)
        m.update({"x_own": xo, "cos_own": co, "sin_own": so,
                  "pad_rep": padrep})
        in_maps.append(m)
    return in_maps, plans


def kernel(**inputs):
    nc = _get_nc()
    in_maps, plans = make_inmaps(inputs)
    res = run_bass_kernel_spmd(nc, in_maps, core_ids=list(range(NCORE)))
    y = np.empty((T, D), np.float32)
    for c in range(NCORE):
        shard = np.asarray(res.results[c]["out_shard"])
        slots, _ = plans[c]
        for j, b in enumerate(slots):
            y[b * QB:b * QB + KB] = shard[2 * j]
            y[b * QB + KB:(b + 1) * QB] = shard[2 * j + 1]
    return y.reshape(1, T, D)
